# version 06
# baseline (speedup 1.0000x reference)
import contextlib
import numpy as np
import ml_dtypes

import concourse.bass as bass
import concourse.mybir as mybir
from concourse.bass_utils import run_bass_kernel_spmd

F32 = mybir.dt.float32
BF16 = mybir.dt.bfloat16
AF = mybir.ActivationFunctionType
ALU = mybir.AluOpType
AX = mybir.AxisListType

B_LOC = 2
S = 2048
D = 1024
DT = D // 128
TT = S // 128
H = 16
HP = H // 2
F = 4096
FT = F // 128
NEXP = 8
GRP = 512
NG = S // GRP
NEG = -1.0e30
EPS = 1e-5

STAGES = ("p0", "pb2", "pb3", "pa", "asm", "moe", "fin")
KB = 1024


SB_BASE = 16512

def B(i, off=0):
    return SB_BASE + 24 * KB + i * 32 * KB + off

def SM(off=0):
    return SB_BASE + 24 * KB + 5 * 32 * KB + off


class Sem:
    __slots__ = ("h", "n", "name")


class Bld:
    ENGS = ("t", "v", "s", "g", "q")

    def __init__(self, nc, es):
        self.nc = nc
        self.es = es
        self.ops = {e: [] for e in self.ENGS}

    def sem(self, name):
        s = Sem()
        s.h = self.es.enter_context(self.nc.semaphore(name))
        s.n = 0
        s.name = name
        return s

    def op(self, eng, fn, waits=(), inc=None, k=1):
        tok = None
        if inc is not None:
            inc.n += k
            tok = (inc, inc.n)
        self.ops[eng].append((tuple(w for w in waits if w is not None), fn,
                              (inc, k) if inc is not None else None))
        return tok

    def dma(self, eng, out, in_, sem, waits=(), **kw):
        def fn(EE, out=out, in_=in_, kw=kw):
            return EE.dma_start(out=out, in_=in_, **kw)
        return self.op(eng, fn, waits, inc=sem, k=16)

    def finish(self, block):
        decos = {"t": block.tensor, "v": block.vector, "s": block.scalar,
                 "g": block.gpsimd, "q": block.sync}
        for e in self.ENGS:
            opl = self.ops[e]
            def make(opl):
                def body(EE):
                    waited = {}
                    for waits, fn, inc in opl:
                        for (sm, val) in waits:
                            if waited.get(id(sm), -1) < val:
                                EE.wait_ge(sm.h, val)
                                waited[id(sm)] = val
                        inst = fn(EE)
                        if inc is not None:
                            inst.then_inc(inc[0].h, inc[1])
                return body
            decos[e](make(opl))


def cdiv(a, b):
    return (a + b - 1) // b


def build(stages=STAGES, taps=(), nbatch=B_LOC):
    nc = bass.Bass("TRN2", target_bir_lowering=False)
    es = contextlib.ExitStack()
    b = Bld(nc, es)

    x_h = nc.dram_tensor("x", [B_LOC, S, D], F32, kind="ExternalInput").ap()
    y_h = nc.dram_tensor("y", [B_LOC, S, D], F32, kind="ExternalOutput").ap()
    wnames_2d = ["Wq", "Wk", "Wv", "Wo", "w_o"]
    W_h = {n: nc.dram_tensor(n, [D, D], F32, kind="ExternalInput").ap() for n in wnames_2d}
    wqkv_h = nc.dram_tensor("w_qkv", [D, 3 * D], F32, kind="ExternalInput").ap()
    vec_h = {n: nc.dram_tensor(n, [D], F32, kind="ExternalInput").ap()
             for n in ["bq", "bk", "bv", "bo", "ln1_g", "ln1_b", "ln2_g", "ln2_b"]}
    wgate_h = nc.dram_tensor("w_gate", [D, NEXP], F32, kind="ExternalInput").ap()
    w1_h = nc.dram_tensor("w1", [NEXP, D, F], F32, kind="ExternalInput").ap()
    b1_h = nc.dram_tensor("b1", [NEXP, F], F32, kind="ExternalInput").ap()
    w2_h = nc.dram_tensor("w2", [NEXP, F, D], F32, kind="ExternalInput").ap()
    b2_h = nc.dram_tensor("b2", [NEXP, D], F32, kind="ExternalInput").ap()

    Ws = {n: nc.dram_tensor(n + "_s", [D, D], BF16, kind="Internal").ap() for n in wnames_2d}
    wqkv_s = nc.dram_tensor("wqkv_s", [D, 3 * D], BF16, kind="Internal").ap()
    w1_s = nc.dram_tensor("w1_s", [NEXP, D, F], BF16, kind="Internal").ap()
    w2_s = nc.dram_tensor("w2_s", [NEXP, F, D], BF16, kind="Internal").ap()
    xres_d = nc.dram_tensor("xres_d", [D, S], BF16, kind="Internal").ap()
    awo_d = nc.dram_tensor("awo_d", [D, S], BF16, kind="Internal").ap()

    eye_bf_h = nc.inline_tensor(np.eye(128, dtype=ml_dtypes.bfloat16), name="c_eye_bf").ap()
    eye_f_h = nc.inline_tensor(np.eye(128, dtype=np.float32), name="c_eye_f").ap()
    cmask_np = np.where(np.tril(np.ones((128, 128))) > 0, 0.0, NEG).astype(np.float32)
    cmask_h = nc.inline_tensor(cmask_np, name="c_cmask").ap()
    ones_row_h = nc.inline_tensor(np.ones((1, 512), dtype=ml_dtypes.bfloat16), name="c_ones_row").ap()
    ones_col_h = nc.inline_tensor(np.ones((1, 128), dtype=ml_dtypes.bfloat16), name="c_ones_col").ap()
    ones_bigcol_h = nc.inline_tensor(np.ones((128, 1), dtype=ml_dtypes.bfloat16), name="c_ones_bigcol").ap()
    eps_col_h = nc.inline_tensor(np.full((128, 1), EPS, dtype=np.float32), name="c_eps_col").ap()

    def sb(name, shape, dtype, off):
        return nc.alloc_sbuf_tensor_at(name, shape, dtype, offset=off)

    p = SB_BASE
    eye_bf = sb("eye_bf", [128, 128], BF16, p); p += 256
    eye_f = sb("eye_f", [128, 128], F32, p); p += 512
    cmask = sb("cmask", [128, 128], F32, p); p += 512
    ones_row = sb("ones_row", [1, 512], BF16, p); p += 1024
    ones_col = sb("ones_col", [1, 128], BF16, p); p += 256
    ones_bigcol = sb("ones_bigcol", [128, 1], BF16, p); p += 32
    eps_col = sb("eps_col", [128, 1], F32, p); p += 32
    vecs = sb("vecs", [128, 64], F32, p); p += 256
    VC = {"bq": 0, "bk": 8, "bo": 16, "ln1_g": 24, "ln1_b": 32, "ln2_g": 40, "ln2_b": 48}
    wg_t = sb("wg_t", [128, DT * NEXP], F32, p); p += 256
    bv_flat = sb("bv_flat", [1, D], BF16, p); p += 2048
    b2e_flat = sb("b2e_flat", [1, 2 * D], BF16, p); p += 4096
    b1e = sb("b1e", [128, 2 * FT], F32, p); p += 256
    qsqh = sb("qsqh", [1, S], BF16, p); p += 4096
    ksqh = sb("ksqh", [1, S], BF16, p); p += 4096
    stats = sb("stats", [128, 64], F32, p); p += 256
    gl_tok = sb("gl_tok", [128, TT * NEXP], F32, p); p += 512
    gate_w = sb("gate_w", [128, TT * NEXP], F32, p); p += 512
    gjunk = sb("gjunk", [128, 24], F32, p); p += 96
    assert p <= SB_BASE + 24 * KB, p

    xnT = sb("xnT", [128, DT * S], BF16, B(0))
    hT = sb("hT", [128, DT * S], BF16, B(1))
    xstage = [sb("xstage0", [128, 4 * D], F32, B(2)),
              sb("xstage1", [128, 4 * D], F32, B(3))]
    p0cx = sb("p0cx", [128, D], F32, SM(0))
    p0junk = sb("p0junk", [128, D], F32, SM(4 * KB))
    p0xn = sb("p0xn", [128, D], BF16, SM(8 * KB))
    p0hn = sb("p0hn", [128, D], BF16, SM(10 * KB))
    oT = sb("oT", [128, DT * S], BF16, B(2))
    qpT = sb("qpT", [128, S], BF16, SM(0))
    kpT = sb("kpT", [128, S], BF16, SM(4 * KB))
    vp = sb("vp", [128, TT * 128], BF16, SM(8 * KB))
    wqkvp = sb("wqkvp", [128, DT * 384], BF16, SM(12 * KB))
    a_bfi = [sb("a_bfi0", [128, S], BF16, B(3)),
             sb("a_bfi1", [128, S], BF16, B(3, 4 * KB)),
             sb("a_bfi2", [128, S], BF16, B(3, 8 * KB)),
             sb("a_bfi3", [128, S], BF16, B(3, 12 * KB))]
    aT = sb("aT", [128, TT * 512], BF16, B(3, 16 * KB))
    wbuf = sb("wbuf", [128, DT * D], BF16, SM(0))
    xres_stage = sb("xres_stage", [128, DT * 512], BF16, B(4))
    QT = sb("QT", [128, DT * S], BF16, B(2))
    KT_ = sb("KT", [128, DT * S], BF16, B(4))
    VA = sb("VA", [128, TT * D], BF16, B(1))
    s_scr2 = [sb("s_scr0", [128, S], F32, B(3)),
              sb("s_scr1", [128, S], F32, B(3, 8 * KB))]
    oa_chunk = sb("oa_chunk", [128, DT * 512], BF16, B(0))
    awo_stage = sb("awo_stage", [128, DT * 512], BF16, B(0, 8 * KB))
    a_bfa = [sb("a_bfa0", [128, S], BF16, B(0, 16 * KB)),
             sb("a_bfa1", [128, S], BF16, B(0, 20 * KB)),
             sb("a_bfa2", [128, S], BF16, B(0, 24 * KB)),
             sb("a_bfa3", [128, S], BF16, B(0, 28 * KB))]
    moechunk = sb("moechunk", [128, DT * 512], F32, B(0))
    c_f = sb("c_f", [128, D], F32, B(0, 16 * KB))
    nrm_f = sb("nrm_f", [128, D], F32, B(0, 20 * KB))
    jnk_f = sb("jnk_f", [128, D], F32, B(0, 24 * KB))
    glT = sb("glT", [8, S], F32, B(2))
    xres_tok = sb("xres_tok", [128, 2 * D], BF16, B(2, 8 * KB))
    awo_tok = sb("awo_tok", [128, 2 * D], BF16, B(2, 12 * KB))
    moeinT = sb("moeinT", [128, DT * S], BF16, B(3))
    acc = sb("acc", [128, TT * D], BF16, B(1))
    h1 = sb("h1", [128, FT * GRP], BF16, B(0))
    w1a = sb("w1a", [128, 4 * F], BF16, B(2))
    w1b = sb("w1b", [128, 4 * F], BF16, B(4))
    w2win = sb("w2win", [128, 8 * D], BF16, SM(0))
    fxstage = [sb("fxstage0", [128, 4 * D], F32, B(2)),
               sb("fxstage1", [128, 4 * D], F32, B(2, 16 * KB))]
    ostage = [sb("ostage0", [128, 4 * D], F32, B(4)),
              sb("ostage1", [128, 4 * D], F32, B(4, 16 * KB))]

    PSA = es.enter_context(nc.psum_tensor("PSA", [128, 8 * 512], F32))

    def bank(i):
        return PSA[:, i * 512:(i + 1) * 512]

    tr_bf = [PSA[:, 6 * 512:7 * 512].bitcast(BF16)[:, 0:512],
             PSA[:, 7 * 512:8 * 512].bitcast(BF16)[:, 0:512]]

    sT = b.sem("sT"); sV = b.sem("sV"); sS = b.sem("sS")
    dcast = b.sem("dcast"); dconst = b.sem("dconst")
    dx = [b.sem("dx0"), b.sem("dx1")]
    dwb = b.sem("dwb"); dwp = b.sem("dwp"); dw1 = b.sem("dw1")
    dw2 = [b.sem("dw2a"), b.sem("dw2b")]
    dsmall = b.sem("dsmall"); dout = b.sem("dout")
    dsp = b.sem("dsp")
    dxt = [b.sem("dxt0"), b.sem("dxt1")]

    b.dma("q", eye_bf[:], eye_bf_h, dconst)
    b.dma("q", eye_f[:], eye_f_h, dconst)
    b.dma("q", cmask[:], cmask_h, dconst)
    b.dma("q", ones_row[:], ones_row_h, dconst)
    b.dma("q", ones_col[:], ones_col_h, dconst)
    b.dma("q", ones_bigcol[:], ones_bigcol_h, dconst)
    b.dma("q", eps_col[:], eps_col_h, dconst)
    for n, c in VC.items():
        b.dma("q", vecs[:, c:c + 8], vec_h[n].rearrange("(k p) -> p k", p=128), dconst,
              allow_slow_non_contiguous=True)
    b.dma("q", wg_t[:].rearrange("p (k e) -> p k e", k=DT),
          wgate_h.rearrange("(k p) e -> p k e", p=128), dconst)
    tok_const = (dconst, dconst.n)

    cast_toks = {}

    def cast(name, dst, src):
        t = b.dma("g", dst, src, dcast, waits=[(dcast, dcast.n)])
        cast_toks[name] = t
        return t

    cast("bv", bv_flat[0:1, :], vec_h["bv"].rearrange("(a d) -> a d", a=1))
    cast("wqkv", wqkv_s, wqkv_h)
    cast("w_o", Ws["w_o"], W_h["w_o"])
    for n in ["Wk", "Wv", "Wq", "Wo"]:
        cast(n, Ws[n], W_h[n])
    big_casts = []
    def record_big_casts(gate_tok):
        first = True
        for e in range(NEXP):
            for nm_, dst, src_ in ((f"w1_{e}", w1_s[e], w1_h[e]),
                                   (f"w2_{e}", w2_s[e], w2_h[e])):
                w_ = [(dcast, dcast.n)]
                if first and gate_tok is not None:
                    w_.append(gate_tok)
                    first = False
                cast_toks[nm_] = b.dma("g", dst, src_, dcast, waits=w_)

    tap_names = []

    def tap(name, ap_, waits):
        if name in taps:
            th = nc.dram_tensor("tap_" + name, list(ap_.shape), ap_.dtype,
                                kind="ExternalOutput").ap()
            b.dma("g", th, ap_, dout, waits=list(waits))
            tap_names.append("tap_" + name)

    tr_free = [None, None]

    def phase_p0(bb, start_tok):
        last_use = [None, None]
        vchain = start_tok
        for c in range(4):
            par = c % 2
            w = [tok_const, start_tok]
            if last_use[par] is not None:
                w.append(last_use[par])
            tok_ld = b.dma("q", xstage[par][:].rearrange("p (tt d) -> p tt d", tt=4),
                           x_h[bb, c * 512:(c + 1) * 512, :].rearrange("(tt p) d -> p tt d", p=128),
                           dx[par], waits=w)
            for tc in range(4):
                t = c * 4 + tc
                xst = xstage[par][:, tc * D:(tc + 1) * D]
                sp = (t % 2) * 8
                sumx = stats[:, sp:sp + 1]; negm = stats[:, sp + 1:sp + 2]
                vsum = stats[:, sp + 2:sp + 3]; rstd = stats[:, sp + 3:sp + 4]
                t1 = b.op("v", lambda E, o=sumx, i=xst: E.reduce_sum(o, i, axis=AX.X),
                          [tok_ld, vchain], inc=sV)
                t2 = b.op("v", lambda E, o=negm, i=sumx: E.tensor_scalar(o, i, -1.0 / D, None, ALU.mult),
                          [t1], inc=sV)
                t3 = b.op("v", lambda E, o=p0cx[:], i=xst, s1=negm: E.tensor_scalar(o, i, s1, None, ALU.add),
                          [t2], inc=sV)
                t4 = b.op("v", lambda E, o=p0junk[:], i=p0cx[:], a=vsum:
                          E.scalar_tensor_tensor(o, i, 1.0, i, ALU.mult, ALU.mult, accum_out=a),
                          [t3], inc=sV)
                t5 = b.op("s", lambda E, o=rstd, i=vsum: E.activation(o, i, AF.Ln, bias=eps_col[:], scale=1.0 / D),
                          [t4], inc=sS)
                t6 = b.op("s", lambda E, o=rstd, i=rstd: E.activation(o, i, AF.Exp, bias=0.0, scale=-0.5),
                          [t5], inc=sS)
                t7 = b.op("v", lambda E, o=p0xn[:], i=p0cx[:], s1=rstd: E.tensor_scalar(o, i, s1, None, ALU.mult),
                          [t6], inc=sV)
                u1 = b.op("v", lambda E, o=sumx, i=p0xn[:]: E.reduce_sum(o, i, axis=AX.X), [t7], inc=sV)
                u2 = b.op("v", lambda E, o=negm, i=sumx: E.tensor_scalar(o, i, -1.0 / D, None, ALU.mult),
                          [u1], inc=sV)
                u3 = b.op("v", lambda E, o=p0cx[:], i=p0xn[:], s1=negm: E.tensor_scalar(o, i, s1, None, ALU.add),
                          [u2], inc=sV)
                u4 = b.op("v", lambda E, o=p0junk[:], i=p0cx[:], a=vsum:
                          E.scalar_tensor_tensor(o, i, 1.0, i, ALU.mult, ALU.mult, accum_out=a),
                          [u3], inc=sV)
                u5 = b.op("s", lambda E, o=rstd, i=vsum: E.activation(o, i, AF.Ln, bias=eps_col[:], scale=1.0 / D),
                          [u4], inc=sS)
                u6 = b.op("s", lambda E, o=rstd, i=rstd: E.activation(o, i, AF.Exp, bias=0.0, scale=-0.5),
                          [u5], inc=sS)
                u7 = b.op("v", lambda E, o=p0hn[:], i=p0cx[:], s1=rstd: E.tensor_scalar(o, i, s1, None, ALU.mult),
                          [u6], inc=sV)
                last_use[par] = t3
                vchain = u7
                for half, srcbuf in [(0, p0xn), (1, p0hn)]:
                    dep = t7 if half == 0 else u7
                    for g4 in range(2):
                        tri = (t * 4 + half * 2 + g4) % 2
                        trt = []
                        for j in range(4):
                            kk = g4 * 4 + j
                            w_ = [dep] if j == 0 else []
                            if j == 0 and tr_free[tri] is not None:
                                w_.append(tr_free[tri])
                            trt.append(b.op("t", lambda E, kk=kk, j=j, tri=tri, srcbuf=srcbuf:
                                            E.transpose(tr_bf[tri][:, j * 128:(j + 1) * 128],
                                                        srcbuf[:, kk * 128:(kk + 1) * 128], eye_bf[:]),
                                            w_, inc=sT))
                        if half == 0:
                            def ev(E, tri=tri, g4=g4, t=t):
                                dst = xnT[:].rearrange("p (k s) -> p k s", k=DT)[:, g4 * 4:g4 * 4 + 4,
                                                                                t * 128:(t + 1) * 128]
                                src = tr_bf[tri][:, 0:512].rearrange("p (j c) -> p j c", j=4)
                                return E.tensor_copy(dst, src)
                            evt = b.op("v", ev, [trt[-1], vchain], inc=sV)
                            tr_free[tri] = evt
                            vchain = evt
                        else:
                            evt = None
                            for j in range(4):
                                kk = g4 * 4 + j
                                def ev1(E, tri=tri, j=j, kk=kk, t=t):
                                    return E.tensor_scalar(
                                        hT[:, kk * S + t * 128: kk * S + (t + 1) * 128],
                                        tr_bf[tri][:, j * 128:(j + 1) * 128],
                                        vecs[:, VC["ln1_g"] + kk:VC["ln1_g"] + kk + 1],
                                        vecs[:, VC["ln1_b"] + kk:VC["ln1_b"] + kk + 1],
                                        ALU.mult, ALU.add)
                                w_ = [trt[-1], vchain] if evt is None else [trt[-1], evt]
                                evt = b.op("v", ev1, w_, inc=sV)
                            tr_free[tri] = evt
                            vchain = evt
        return vchain

    def phase_pb2(bb, p0_done):
        prev_pair_done = None
        sbank = [None, None, None, None]
        statg = [None, None, None, None]
        abf_free = [None, None, None, None]
        hp_count = 1 if "pb2short" in stages else HP
        for hp in range(hp_count):
            wts = [cast_toks["wqkv"], (dwp, dwp.n)]
            if prev_pair_done is not None:
                wts.append(prev_pair_done)
            tok_w = None
            for part in range(3):
                col0 = part * D + hp * 128
                tok_w = b.dma("q",
                              wqkvp[:].rearrange("p (k c) -> p k c", k=DT)[:, :, part * 128:(part + 1) * 128],
                              wqkv_s.rearrange("(k p) n -> p k n", p=128)[:, :, col0:col0 + 128],
                              dwp, waits=wts if part == 0 else [])
            pb_free = [None, None]
            prod = {}
            for which, dstbuf in [(0, qpT), (1, kpT)]:
                for nck in range(4):
                    pbk = (which * 4 + nck) % 2
                    mmw = [tok_w, p0_done]
                    if pb_free[pbk] is not None:
                        mmw.append(pb_free[pbk])
                    if prev_pair_done is not None:
                        mmw.append(prev_pair_done)
                    tmm = None
                    for k in range(DT):
                        def mm(E, k=k, which=which, nck=nck, pbk=pbk):
                            return E.matmul(bank(pbk),
                                            wqkvp[:, k * 384 + which * 128: k * 384 + (which + 1) * 128],
                                            hT[:, k * S + nck * 512: k * S + (nck + 1) * 512],
                                            start=(k == 0), stop=(k == DT - 1))
                        tmm = b.op("t", mm, mmw if k == 0 else [], inc=sT if k == DT - 1 else None)
                    ev = b.op("v", lambda E, d=dstbuf[:, nck * 512:(nck + 1) * 512], pbk=pbk:
                              E.tensor_copy(d, bank(pbk)), [tmm], inc=sV)
                    pb_free[pbk] = ev
                    prod[(which, nck)] = ev
            vtok = None
            for mt in range(TT):
                pbk = mt % 2
                mmw = [tok_w, p0_done]
                if pb_free[pbk] is not None:
                    mmw.append(pb_free[pbk])
                tmm = None
                for k in range(DT):
                    def mm(E, k=k, mt=mt, pbk=pbk):
                        return E.matmul(bank(pbk)[:, 0:128],
                                        hT[:, k * S + mt * 128: k * S + (mt + 1) * 128],
                                        wqkvp[:, k * 384 + 256: k * 384 + 384],
                                        start=(k == 0), stop=(k == DT - 1))
                    tmm = b.op("t", mm, mmw if k == 0 else [], inc=sT if k == DT - 1 else None)
                ev = b.op("v", lambda E, mt=mt, pbk=pbk:
                          E.tensor_copy(vp[:, mt * 128:(mt + 1) * 128], bank(pbk)[:, 0:128]),
                          [tmm], inc=sV)
                pb_free[pbk] = ev
                vtok = ev
            qkv_done = [prod[(0, 3)], prod[(1, 3)], vtok]
            o_last = None
            abf_free = [None, None, None, None]
            sbank = [None, None, None, None]
            statg = [None, None, None, None]
            srot = 0
            prev_o = prev_pair_done
            for hh in range(2):
                pr = 64 * hh
                for qc in range(4):
                    ktc = 4 * qc + 4
                    atr_evs = []
                    for q4 in range(4):
                        qi = qc * 4 + q4
                        for kt in range(qi + 1, ktc):
                            z = b.op("v", lambda E, kt=kt, q4=q4:
                                     E.memset(aT[:, kt * 512 + q4 * 128: kt * 512 + (q4 + 1) * 128], 0.0),
                                     [prev_o], inc=sV)
                            atr_evs.append(z)
                    for q4 in range(4):
                        qi = qc * 4 + q4
                        wkt = qi + 1
                        wid = wkt * 128
                        nch = cdiv(wid, 512)
                        par = q4 % 4
                        sb_ = 16 + par * 12
                        maxc = [stats[:, sb_ + i:sb_ + i + 1] for i in range(4)]
                        gmax = stats[:, sb_ + 4:sb_ + 5]
                        negms = stats[:, sb_ + 5:sb_ + 6]
                        denc = [stats[:, sb_ + 6 + i:sb_ + 7 + i] for i in range(4)]
                        dens = stats[:, sb_ + 10:sb_ + 11]
                        rr = stats[:, sb_ + 11:sb_ + 12]
                        dgb = (qi * 128) // 512
                        bk = {ci: 2 + (srot + ci) % 4 for ci in range(nch)}
                        mm_ci = {}
                        for ci in range(nch):
                            nw = min(512, wid - ci * 512)
                            def mm(E, ci=ci, nw=nw, qi=qi, pr=pr, bb_=bk[ci]):
                                return E.matmul(PSA[:, bb_ * 512:bb_ * 512 + nw],
                                                qpT[pr:pr + 64, qi * 128:(qi + 1) * 128],
                                                kpT[pr:pr + 64, ci * 512:ci * 512 + nw],
                                                start=True, stop=True)
                            w_ = [sbank[bk[ci] - 2]] + (qkv_done if (ci == 0 and qc == 0 and hh == 0) else [])
                            mm_ci[ci] = b.op("t", mm, w_, inc=sT)
                        for _w in range(24):
                            b.op("t", lambda E: E.ldweights(eye_bf[:]), [])
                        mt1 = b.op("v", lambda E, qi=qi, bb_=bk[dgb]:
                                   E.tensor_tensor(PSA[:, bb_ * 512 + (qi * 128) % 512:
                                                       bb_ * 512 + (qi * 128) % 512 + 128],
                                                   PSA[:, bb_ * 512 + (qi * 128) % 512:
                                                       bb_ * 512 + (qi * 128) % 512 + 128],
                                                   cmask[:], ALU.add),
                                   [mm_ci[dgb], statg[par]], inc=sV)
                        w_ = [mm_ci[0], statg[par]]
                        if dgb == 0:
                            w_.append(mt1)
                        nw0 = min(512, wid)
                        gm = b.op("v", lambda E, o=gmax, nw0=nw0, bb_=bk[0]:
                                  E.reduce_max(o, PSA[:, bb_ * 512:bb_ * 512 + nw0],
                                               axis=AX.X, negate=True), w_, inc=sV)
                        ng = b.op("v", lambda E, o=negms, i=gmax:
                                  E.tensor_scalar(o, i, 0.125, None, ALU.mult), [gm], inc=sV)
                        abuf = a_bfi[par]
                        e_toks = []
                        for ci in range(nch):
                            nw = min(512, wid - ci * 512)
                            w_ = [ng, mm_ci[ci]]
                            if ci == dgb:
                                w_.append(mt1)
                            if ci == 0:
                                w_.append(abf_free[par])
                            def ex(E, ci=ci, nw=nw, abuf=abuf, bias=negms, acc_=denc[ci], bb_=bk[ci]):
                                return E.activation(abuf[:, ci * 512:ci * 512 + nw],
                                                    PSA[:, bb_ * 512:bb_ * 512 + nw],
                                                    AF.Exp, bias=bias, scale=0.125, accum_out=acc_)
                            et = b.op("s", ex, w_, inc=sS)
                            e_toks.append(et)
                            sbank[bk[ci] - 2] = et
                        srot = (srot + nch) % 4
                        ds = b.op("v", lambda E, o=dens, nch=nch, sb_=sb_:
                                  E.reduce_sum(o, stats[:, sb_ + 6:sb_ + 6 + nch], axis=AX.X),
                                  e_toks, inc=sV)
                        rv = b.op("v", lambda E, o=rr, i=dens: E.reciprocal(o, i), [ds], inc=sV)
                        nm = b.op("v", lambda E, abuf=abuf, wid=wid, s1=rr:
                                  E.tensor_scalar(abuf[:, 0:wid], abuf[:, 0:wid], s1, None, ALU.mult),
                                  [rv], inc=sV)
                        statg[par] = nm
                        for g4 in range(cdiv(wkt, 4)):
                            nblk = min(4, wkt - g4 * 4)
                            tri = (qi + g4 + hh) % 2
                            trt = []
                            for j in range(nblk):
                                kb = g4 * 4 + j
                                w_ = [nm] if j == 0 else []
                                if j == 0 and tr_free[tri] is not None:
                                    w_.append(tr_free[tri])
                                trt.append(b.op("t", lambda E, kb=kb, j=j, tri=tri, abuf=abuf:
                                                E.transpose(tr_bf[tri][:, j * 128:(j + 1) * 128],
                                                            abuf[:, kb * 128:(kb + 1) * 128], eye_bf[:]),
                                                w_, inc=sT))
                            def ev(E, tri=tri, g4=g4, q4=q4, nblk=nblk):
                                dst = aT[:].rearrange("p (k c) -> p k c", k=TT)[:, g4 * 4:g4 * 4 + nblk,
                                                                               q4 * 128:(q4 + 1) * 128]
                                src = tr_bf[tri][:, 0:nblk * 128].rearrange("p (j c) -> p j c", j=nblk)
                                return E.tensor_copy(dst, src)
                            evt = b.op("v", ev, [trt[-1], prev_o], inc=sV)
                            tr_free[tri] = evt
                            atr_evs.append(evt)
                        abf_free[par] = atr_evs[-1]
                    pbk = (qc + hh) % 2
                    mmw = atr_evs + [pb_free[pbk]]
                    tmm = None
                    for kt in range(ktc):
                        if hh == 0:
                            def mm(E, kt=kt, pr=pr, pbk=pbk, ktc=ktc):
                                return E.matmul(PSA[0:64, pbk * 512:(pbk + 1) * 512],
                                                vp[:, kt * 128 + pr: kt * 128 + pr + 64],
                                                aT[:, kt * 512:(kt + 1) * 512],
                                                start=(kt == 0), stop=(kt == ktc - 1))
                        else:
                            def mm(E, kt=kt, pr=pr, pbk=pbk, ktc=ktc):
                                return E.matmul(PSA[64:128, pbk * 512:(pbk + 1) * 512],
                                                vp[:, kt * 128 + pr: kt * 128 + pr + 64],
                                                aT[:, kt * 512:(kt + 1) * 512],
                                                start=(kt == 0), stop=(kt == ktc - 1),
                                                tile_position=(0, 64))
                        tmm = b.op("t", mm, mmw if kt == 0 else [], inc=sT if kt == ktc - 1 else None)
                    prev_o = tmm
                    def ev(E, hp=hp, pr=pr, qc=qc, pbk=pbk, hh=hh):
                        src = PSA[0:64, pbk * 512:(pbk + 1) * 512] if hh == 0 else \
                              PSA[64:128, pbk * 512:(pbk + 1) * 512]
                        return E.tensor_copy(oT[pr:pr + 64, hp * S + qc * 512: hp * S + (qc + 1) * 512], src)
                    evt = b.op("v", ev, [tmm], inc=sV)
                    pb_free[pbk] = evt
                    o_last = evt
            prev_pair_done = o_last
        return prev_pair_done

    def phase_pb3(bb, pb2_done, p0_done):
        tok_w = b.dma("q", wbuf[:].rearrange("p (k n) -> p k n", k=DT),
                      Ws["w_o"].rearrange("(k p) n -> p k n", p=128), dwb,
                      waits=[cast_toks["w_o"], pb2_done, (dwb, dwb.n)])
        pb_free = [None, None]
        last_wr = None
        for ncc in range(4):
            ch_evs = []
            for m in range(DT):
                pbk = (ncc * DT + m) % 2
                mmw = [tok_w, pb2_done, p0_done, pb_free[pbk]]
                if m == 0 and last_wr is not None:
                    mmw.append(last_wr)
                tmm = None
                for k in range(DT):
                    def mm(E, k=k, m=m, ncc=ncc, pbk=pbk):
                        return E.matmul(bank(pbk),
                                        wbuf[:, k * D + m * 128: k * D + (m + 1) * 128],
                                        oT[:, k * S + ncc * 512: k * S + (ncc + 1) * 512],
                                        start=(k == 0), stop=(k == DT - 1))
                    tmm = b.op("t", mm, mmw if k == 0 else [], inc=sT if k == DT - 1 else None)
                def ev(E, m=m, ncc=ncc, pbk=pbk):
                    return E.tensor_tensor(xres_stage[:, m * 512:(m + 1) * 512],
                                           bank(pbk),
                                           xnT[:, m * S + ncc * 512: m * S + (ncc + 1) * 512], ALU.add)
                w_ = [tmm]
                if last_wr is not None:
                    w_.append(last_wr)
                evt = b.op("v", ev, w_, inc=sV)
                pb_free[pbk] = evt
                ch_evs.append(evt)
            last_wr = b.dma("q", xres_d.rearrange("(k p) s -> p k s", p=128)[:, :, ncc * 512:(ncc + 1) * 512],
                            xres_stage[:].rearrange("p (k c) -> p k c", k=DT),
                            dsp, waits=ch_evs + [(dsp, dsp.n)])
        return last_wr

    def phase_pa(bb, pb3_done, p0_done):
        projdone = {}
        prev = pb3_done
        pb_free = [None, None]
        for wn, kind in [("Wk", "K"), ("Wv", "V"), ("Wq", "Q")]:
            tok_w = b.dma("q", wbuf[:].rearrange("p (k n) -> p k n", k=DT),
                          Ws[wn].rearrange("(k p) n -> p k n", p=128), dwb,
                          waits=[cast_toks[wn], prev, (dwb, dwb.n)])
            last = None
            if kind in ("K", "Q"):
                dst = KT_ if kind == "K" else QT
                bc = VC["bk"] if kind == "K" else VC["bq"]
                for ncc in range(4):
                    for m in range(DT):
                        pbk = (ncc * DT + m) % 2
                        mmw = [tok_w, p0_done, pb_free[pbk], pb3_done]
                        tmm = None
                        for k in range(DT):
                            def mm(E, k=k, m=m, ncc=ncc, pbk=pbk):
                                return E.matmul(bank(pbk),
                                                wbuf[:, k * D + m * 128: k * D + (m + 1) * 128],
                                                xnT[:, k * S + ncc * 512: k * S + (ncc + 1) * 512],
                                                start=(k == 0), stop=(k == DT - 1))
                            tmm = b.op("t", mm, mmw if k == 0 else [], inc=sT if k == DT - 1 else None)
                        def ev(E, m=m, ncc=ncc, pbk=pbk, bc=bc, dst=dst):
                            return E.activation(dst[:, m * S + ncc * 512: m * S + (ncc + 1) * 512],
                                                bank(pbk), AF.Identity,
                                                bias=vecs[:, bc + m:bc + m + 1], scale=1.0)
                        evt = b.op("s", ev, [tmm], inc=sS)
                        pb_free[pbk] = evt
                        last = evt
            else:
                for mt in range(TT):
                    for ncc in range(2):
                        pbk = (mt * 2 + ncc) % 2
                        mmw = [tok_w, p0_done, pb_free[pbk], cast_toks["bv"], tok_const]
                        tmm = None
                        for k in range(DT):
                            def mm(E, k=k, mt=mt, ncc=ncc, pbk=pbk):
                                return E.matmul(bank(pbk),
                                                xnT[:, k * S + mt * 128: k * S + (mt + 1) * 128],
                                                wbuf[:, k * D + ncc * 512: k * D + (ncc + 1) * 512],
                                                start=(k == 0), stop=False)
                            tmm = b.op("t", mm, mmw if k == 0 else [], inc=None)
                        tmm = b.op("t", lambda E, ncc=ncc, pbk=pbk:
                                   E.matmul(bank(pbk), ones_col[0:1, :],
                                            bv_flat[0:1, ncc * 512:(ncc + 1) * 512],
                                            start=False, stop=True), [], inc=sT)
                        evt = b.op("v", lambda E, mt=mt, ncc=ncc, pbk=pbk:
                                   E.tensor_copy(VA[:, mt * D + ncc * 512: mt * D + (ncc + 1) * 512],
                                                 bank(pbk)), [tmm], inc=sV)
                        pb_free[pbk] = evt
                        last = evt
            projdone[kind] = last
            prev = last

        sq_ev = {}
        for which, (srcb, dsth, pd) in enumerate([(QT, qsqh, "Q"), (KT_, ksqh, "K")]):
            prev_mm = None
            evt = None
            for k in range(DT):
                tqw = [projdone[pd], prev_mm, projdone["V"]]
                tq = b.op("v", lambda E, k=k, srcb=srcb:
                          E.tensor_tensor(s_scr2[0][:].bitcast(BF16)[:, 0:S],
                                          srcb[:, k * S:(k + 1) * S],
                                          srcb[:, k * S:(k + 1) * S], ALU.mult),
                          tqw, inc=sV)
                tmm = None
                for ncc in range(4):
                    def mm(E, ncc=ncc, k=k):
                        return E.matmul(PSA[0:1, (2 + ncc) * 512:(2 + ncc) * 512 + 512],
                                        ones_bigcol[:, 0:1],
                                        s_scr2[0][:].bitcast(BF16)[:, ncc * 512:(ncc + 1) * 512],
                                        start=(k == 0), stop=(k == DT - 1))
                    tmm = b.op("t", mm, [tq] if ncc == 0 else [], inc=sT if ncc == 3 else None)
                prev_mm = tmm
            for ncc in range(4):
                evt = b.op("v", lambda E, ncc=ncc, dsth=dsth:
                           E.tensor_scalar(dsth[0:1, ncc * 512:(ncc + 1) * 512],
                                           PSA[0:1, (2 + ncc) * 512:(2 + ncc) * 512 + 512],
                                           -0.5, None, ALU.mult), [prev_mm], inc=sV)
            sq_ev[which] = evt
        sq_all = [sq_ev[0], sq_ev[1]]
        sq_mm_done = sq_ev[1]

        tok_wo = b.dma("q", wbuf[:].rearrange("p (k n) -> p k n", k=DT),
                       Ws["Wo"].rearrange("(k p) n -> p k n", p=128), dwb,
                       waits=[cast_toks["Wo"], projdone["V"], (dwb, dwb.n)])

        abf_free = [None, None, None, None]
        awo_last = None
        pb2f = [None, None]
        sbankA = [sq_mm_done, sq_mm_done, sq_mm_done, sq_mm_done]
        scr_free = [None, None]
        statg = [None, None, None, None]
        prev_o = pb3_done
        for qc in range(4):
            atr_evs_c = []
            for q4 in range(4):
                qi = qc * 4 + q4
                par = q4 % 4
                sb_ = 16 + par * 12
                negm = stats[:, sb_:sb_ + 1]
                den = stats[:, sb_ + 1:sb_ + 2]
                rr = stats[:, sb_ + 2:sb_ + 3]
                Xs = s_scr2[q4 % 2]
                mm_ci = {}
                for ci in range(4):
                    mmw = [sbankA[ci]]
                    if ci == 0 and qi == 0:
                        mmw += [projdone["Q"], projdone["K"]] + sq_all
                    tmm = None
                    for k in range(DT):
                        def mm(E, ci=ci, k=k, qi=qi):
                            return E.matmul(bank(2 + ci),
                                            QT[:, k * S + qi * 128: k * S + (qi + 1) * 128],
                                            KT_[:, k * S + ci * 512: k * S + (ci + 1) * 512],
                                            start=(k == 0), stop=False)
                        tmm = b.op("t", mm, mmw if k == 0 else [], inc=None)
                    tmm = b.op("t", lambda E, ci=ci:
                               E.matmul(bank(2 + ci), ones_col[0:1, :],
                                        ksqh[0:1, ci * 512:(ci + 1) * 512], start=False, stop=False),
                               [], inc=None)
                    tmm = b.op("t", lambda E, ci=ci, qi=qi:
                               E.matmul(bank(2 + ci), qsqh[0:1, qi * 128:(qi + 1) * 128],
                                        ones_row[0:1, :], start=False, stop=True),
                               [], inc=sT)
                    mm_ci[ci] = tmm
                for _w in range(16):
                    b.op("t", lambda E: E.ldweights(eye_bf[:]), [])
                e1_toks = []
                for ci in range(4):
                    w_ = [mm_ci[ci]]
                    if ci == 0:
                        w_.append(scr_free[q4 % 2])
                    def ex(E, ci=ci, Xs=Xs):
                        return E.activation(Xs[:, ci * 512:(ci + 1) * 512],
                                            PSA[:, (2 + ci) * 512:(3 + ci) * 512],
                                            AF.Exp, bias=0.0, scale=2.0)
                    et = b.op("s", ex, w_, inc=sS)
                    e1_toks.append(et)
                    sbankA[ci] = et
                abuf = a_bfa[par]
                e3 = b.op("s", lambda E, abuf=abuf, Xs=Xs, acc_=den:
                          E.activation(abuf[:], Xs[:], AF.Exp, bias=0.0, scale=1.0, accum_out=acc_),
                          e1_toks + [statg[par], abf_free[par]], inc=sS)
                scr_free[q4 % 2] = e3
                e4 = b.op("v", lambda E, o=rr, i=den: E.reciprocal(o, i), [e3], inc=sV)
                e5 = b.op("v", lambda E, abuf=abuf, s1=rr:
                          E.tensor_scalar(abuf[:], abuf[:], s1, None, ALU.mult), [e4], inc=sV)
                statg[par] = e5
                for g4 in range(4):
                    tri = (qi + g4) % 2
                    trt = []
                    for j in range(4):
                        kb = g4 * 4 + j
                        w_ = [e5] if j == 0 else []
                        if j == 0 and tr_free[tri] is not None:
                            w_.append(tr_free[tri])
                        trt.append(b.op("t", lambda E, kb=kb, j=j, tri=tri, abuf=abuf:
                                        E.transpose(tr_bf[tri][:, j * 128:(j + 1) * 128],
                                                    abuf[:, kb * 128:(kb + 1) * 128], eye_bf[:]),
                                        w_, inc=sT))
                    def ev(E, tri=tri, g4=g4, q4=q4):
                        dst = aT[:].rearrange("p (k c) -> p k c", k=TT)[:, g4 * 4:g4 * 4 + 4,
                                                                       q4 * 128:(q4 + 1) * 128]
                        src = tr_bf[tri][:, 0:512].rearrange("p (j c) -> p j c", j=4)
                        return E.tensor_copy(dst, src)
                    evt = b.op("v", ev, [trt[-1], prev_o], inc=sV)
                    tr_free[tri] = evt
                    atr_evs_c.append(evt)
                abf_free[par] = atr_evs_c[-1]
            o_evs = []
            o_mm_last = None
            for m in range(DT):
                pbk = m % 2
                mmw = [atr_evs_c[-1], projdone["V"], pb2f[pbk]]
                tmm = None
                for kt in range(TT):
                    def mm(E, kt=kt, m=m, pbk=pbk):
                        return E.matmul(bank(pbk),
                                        VA[:, kt * D + m * 128: kt * D + (m + 1) * 128],
                                        aT[:, kt * 512:(kt + 1) * 512],
                                        start=(kt == 0), stop=(kt == TT - 1))
                    tmm = b.op("t", mm, mmw if kt == 0 else [], inc=sT if kt == TT - 1 else None)
                o_mm_last = tmm
                evt = b.op("v", lambda E, m=m, pbk=pbk:
                           E.tensor_copy(oa_chunk[:, m * 512:(m + 1) * 512], bank(pbk)), [tmm], inc=sV)
                pb2f[pbk] = evt
                o_evs.append(evt)
            st_evs = []
            for m in range(DT):
                pbk = m % 2
                mmw = [o_evs[-1], tok_wo, pb2f[pbk]]
                if m == 0 and awo_last is not None:
                    mmw.append(awo_last)
                tmm = None
                for k in range(DT):
                    def mm(E, k=k, m=m, pbk=pbk):
                        return E.matmul(bank(pbk),
                                        wbuf[:, k * D + m * 128: k * D + (m + 1) * 128],
                                        oa_chunk[:, k * 512:(k + 1) * 512],
                                        start=(k == 0), stop=(k == DT - 1))
                    tmm = b.op("t", mm, mmw if k == 0 else [], inc=sT if k == DT - 1 else None)
                def ev(E, m=m, pbk=pbk):
                    return E.activation(awo_stage[:, m * 512:(m + 1) * 512],
                                        bank(pbk), AF.Identity,
                                        bias=vecs[:, VC["bo"] + m:VC["bo"] + m + 1], scale=1.0)
                w_ = [tmm]
                if awo_last is not None:
                    w_.append(awo_last)
                evt = b.op("s", ev, w_, inc=sS)
                pb2f[pbk] = evt
                st_evs.append(evt)
            awo_last = b.dma("q", awo_d.rearrange("(k p) s -> p k s", p=128)[:, :, qc * 512:(qc + 1) * 512],
                             awo_stage[:].rearrange("p (k c) -> p k c", k=DT),
                             dsp, waits=st_evs + [(dsp, dsp.n)])
            prev_o = o_mm_last
        return awo_last

    def phase_asm(bb, pa_done, pb3_done):
        gl_evs = []
        moein_last = None
        acc_toks = {}
        vchain = None
        prev_chunk_reads = []
        stg_free = {}
        for cc in range(4):
            ch_evs = []
            this_chunk_first = True
            for t4 in range(4):
                t = cc * 4 + t4
                par = t % 2
                ldw = [pb3_done, pa_done, stg_free.get(("x", par)), (dxt[par], dxt[par].n)]
                tok_x = b.op("q", lambda E, t=t, par=par:
                             E.dma_start_transpose(xres_tok[:, par * D:(par + 1) * D],
                                                   xres_d[:, t * 128:(t + 1) * 128]),
                             ldw, inc=dxt[par], k=16)
                tok_a = b.op("q", lambda E, t=t, par=par:
                             E.dma_start_transpose(awo_tok[:, par * D:(par + 1) * D],
                                                   awo_d[:, t * 128:(t + 1) * 128]),
                             [stg_free.get(("a", par))], inc=dxt[par], k=16)
                xtok = xres_tok[:, par * D:(par + 1) * D]
                atok = awo_tok[:, par * D:(par + 1) * D]
                a1 = b.op("v", lambda E, t=t, xtok=xtok, atok=atok:
                          E.tensor_tensor(acc[:, t * D:(t + 1) * D], xtok, atok, ALU.add),
                          [tok_x, tok_a, vchain], inc=sV)
                acc_toks[t] = a1
                stg_free[("a", par)] = a1
                vchain = a1
                if "noln" in stages:
                    stg_free[("x", par)] = a1
                    ch_evs.append(a1)
                    continue
                sumx = stats[:, 0:1]; negm = stats[:, 1:2]; vsum = stats[:, 2:3]; rstd = stats[:, 3:4]
                l1 = b.op("v", lambda E, o=sumx, xtok=xtok: E.reduce_sum(o, xtok, axis=AX.X), [a1], inc=sV)
                l2 = b.op("v", lambda E, o=negm, i=sumx: E.tensor_scalar(o, i, -1.0 / D, None, ALU.mult),
                          [l1], inc=sV)
                l3 = b.op("v", lambda E, o=c_f[:], s1=negm, xtok=xtok:
                          E.tensor_scalar(o, xtok, s1, None, ALU.add), [l2], inc=sV)
                stg_free[("x", par)] = l3
                l4 = b.op("v", lambda E, o=jnk_f[:], i=c_f[:], a=vsum:
                          E.scalar_tensor_tensor(o, i, 1.0, i, ALU.mult, ALU.mult, accum_out=a),
                          [l3], inc=sV)
                l5 = b.op("s", lambda E, o=rstd, i=vsum: E.activation(o, i, AF.Ln, bias=eps_col[:], scale=1.0 / D),
                          [l4], inc=sS)
                l6 = b.op("s", lambda E, o=rstd, i=rstd: E.activation(o, i, AF.Exp, bias=0.0, scale=-0.5),
                          [l5], inc=sS)
                l7 = b.op("v", lambda E, o=nrm_f[:], i=c_f[:], s1=rstd: E.tensor_scalar(o, i, s1, None, ALU.mult),
                          [l6], inc=sV)
                vchain = l7
                if "noltr" in stages:
                    stg_free[("x", par)] = l3
                    ch_evs.append(l7)
                    continue
                for g4 in range(2):
                    tri = (t * 2 + g4) % 2
                    trt = []
                    for j in range(4):
                        kk = g4 * 4 + j
                        w_ = [l7] if j == 0 else []
                        if j == 0 and tr_free[tri] is not None:
                            w_.append(tr_free[tri])
                        if j == 0 and this_chunk_first:
                            w_ += prev_chunk_reads
                        trt.append(b.op("t", lambda E, kk=kk, j=j, tri=tri:
                                        E.transpose(PSA[:, (6 + tri) * 512 + j * 128:(6 + tri) * 512 + (j + 1) * 128],
                                                    nrm_f[:, kk * 128:(kk + 1) * 128], eye_f[:]),
                                        w_, inc=sT))
                    this_chunk_first = False
                    evt = None
                    for j in range(4):
                        kk = g4 * 4 + j
                        if "nomoee" in stages:
                            def ev(E, tri=tri, j=j, kk=kk, t4=t4):
                                return E.tensor_copy(
                                    moechunk[:, kk * 512 + t4 * 128: kk * 512 + (t4 + 1) * 128],
                                    PSA[:, (6 + tri) * 512 + j * 128:(6 + tri) * 512 + (j + 1) * 128])
                        else:
                            def ev(E, tri=tri, j=j, kk=kk, t4=t4):
                                return E.tensor_scalar(
                                    moechunk[:, kk * 512 + t4 * 128: kk * 512 + (t4 + 1) * 128],
                                    PSA[:, (6 + tri) * 512 + j * 128:(6 + tri) * 512 + (j + 1) * 128],
                                    vecs[:, VC["ln2_g"] + kk:VC["ln2_g"] + kk + 1],
                                    vecs[:, VC["ln2_b"] + kk:VC["ln2_b"] + kk + 1],
                                    ALU.mult, ALU.add)
                        w_ = [trt[-1], vchain] if evt is None else [trt[-1], evt]
                        evt = b.op("v", ev, w_, inc=sV)
                    tr_free[tri] = evt
                    vchain = evt
                    ch_evs.append(evt)
            if "noln" in stages or "noltr" in stages:
                gl_evs.append(None)
                moein_last = ch_evs[-1]
                continue
            g1t = None
            for k in range(DT):
                def mm(E, k=k):
                    return E.matmul(PSA[0:8, 0:512], wg_t[:, k * NEXP:(k + 1) * NEXP],
                                    moechunk[:, k * 512:(k + 1) * 512],
                                    start=(k == 0), stop=(k == DT - 1))
                g1t = b.op("t", mm, ([tok_const] + ch_evs) if k == 0 else [],
                           inc=sT if k == DT - 1 else None)
            g2t = b.op("v", lambda E, cc=cc: E.tensor_copy(glT[0:8, cc * 512:(cc + 1) * 512],
                                                           PSA[0:8, 0:512]), [g1t, vchain], inc=sV)
            gl_evs.append(g2t)
            vchain = g2t
            prev_chunk_reads = [g1t]
            for k in range(DT):
                def cp(E, k=k, cc=cc):
                    return E.tensor_copy(moeinT[:, k * S + cc * 512: k * S + (cc + 1) * 512],
                                         moechunk[:, k * 512:(k + 1) * 512])
                moein_last = b.op("v", cp, [ch_evs[-1], vchain], inc=sV)
                vchain = moein_last
            prev_chunk_reads.append(moein_last)
        gate_toks = {}
        if "nogate" in stages:
            return moein_last, gate_toks, acc_toks, vchain
        for t in range(TT):
            w_ = [gl_evs[t // 4], tr_free[0], vchain]
            tt_ = b.op("t", lambda E, t=t: E.transpose(PSA[:, 6 * 512:6 * 512 + 8],
                                                       glT[0:8, t * 128:(t + 1) * 128],
                                                       eye_f[0:8, 0:8]), w_, inc=sT)
            gv = b.op("v", lambda E, t=t: E.tensor_copy(gl_tok[:, t * NEXP:(t + 1) * NEXP],
                                                        PSA[:, 6 * 512:6 * 512 + 8]), [tt_], inc=sV)
            tr_free[0] = gv
            vchain = gv
            glt = gl_tok[:, t * NEXP:(t + 1) * NEXP]
            negm1 = stats[:, 4:5]; m2 = stats[:, 5:6]; ed = stats[:, 6:7]; rr2 = stats[:, 7:8]
            q1 = b.op("v", lambda E, o=negm1, i=glt: E.reduce_max(o, i, axis=AX.X, negate=True),
                      [gv, vchain], inc=sV)
            q2 = b.op("v", lambda E, i=glt, s1=negm1: E.tensor_scalar(gjunk[:, 0:NEXP], i, s1, None, ALU.add),
                      [q1], inc=sV)
            q4 = b.op("v", lambda E: E.tensor_scalar(gjunk[:, 8:8 + NEXP], gjunk[:, 0:NEXP], 0.0, None, ALU.is_ge),
                      [q2], inc=sV)
            q5 = b.op("v", lambda E, i=glt: E.scalar_tensor_tensor(gjunk[:, 16:16 + NEXP], gjunk[:, 8:8 + NEXP],
                                                                   NEG, i, ALU.mult, ALU.add),
                      [q4], inc=sV)
            q6 = b.op("v", lambda E, o=m2: E.reduce_max(o, gjunk[:, 16:16 + NEXP], axis=AX.X), [q5], inc=sV)
            q7 = b.op("v", lambda E, i=glt, s1=m2: E.tensor_scalar(gjunk[:, 8:8 + NEXP], i, s1, None, ALU.is_ge),
                      [q6], inc=sV)
            q8 = b.op("s", lambda E, i=glt, bias=negm1: E.activation(gjunk[:, 16:16 + NEXP], i, AF.Exp,
                                                                    bias=bias, scale=1.0),
                      [q6], inc=sS)
            q9 = b.op("s", lambda E, i=m2, bias=negm1, o=ed: E.activation(o, i, AF.Exp, bias=bias, scale=1.0),
                      [q6], inc=sS)
            q10 = b.op("v", lambda E, o=ed, i=ed: E.tensor_scalar(o, i, 1.0, None, ALU.add), [q9], inc=sV)
            q11 = b.op("v", lambda E, o=rr2, i=ed: E.reciprocal(o, i), [q10], inc=sV)
            q12 = b.op("v", lambda E: E.tensor_tensor(gjunk[:, 16:16 + NEXP], gjunk[:, 16:16 + NEXP],
                                                      gjunk[:, 8:8 + NEXP], ALU.mult), [q7, q8], inc=sV)
            q13 = b.op("v", lambda E, t=t, s1=rr2: E.tensor_scalar(gate_w[:, t * NEXP:(t + 1) * NEXP],
                                                                   gjunk[:, 16:16 + NEXP], s1, None, ALU.mult),
                       [q11, q12], inc=sV)
            gate_toks[t] = q13
            vchain = q13
        return moein_last, gate_toks, acc_toks, vchain

    def phase_moe(bb, moein_done, gate_toks, acc_toks, asm_done):
        w1_use_done = None
        last_acc = dict(acc_toks)
        prev_w2_evs = None
        w2slot_free = None
        nexp_run = 1 if "moe1" in stages else NEXP
        bias_done = {}
        for e in range(nexp_run):
            w1w = [cast_toks[f"w1_{e}"], (dw1, dw1.n)]
            if e == 0:
                w1w += [moein_done, asm_done]
            if w1_use_done is not None:
                w1w.append(w1_use_done)
            b.dma("q", w1a[:].rearrange("p (k f) -> p k f", k=4),
                  w1_s[e].rearrange("(k p) f -> p k f", p=128)[:, 0:4, :], dw1, waits=w1w)
            tok_w1 = b.dma("q", w1b[:].rearrange("p (k f) -> p k f", k=4),
                           w1_s[e].rearrange("(k p) f -> p k f", p=128)[:, 4:8, :], dw1)
            sl = e % 2
            tok_b1 = b.dma("q", b1e[:, sl * FT:(sl + 1) * FT],
                           b1_h[e].rearrange("(k p) -> p k", p=128), dsmall,
                           waits=[(dsmall, dsmall.n)], allow_slow_non_contiguous=True)
            b2w = [(dcast, dcast.n), (dsmall, dsmall.n)]
            if e >= 2:
                b2w.append(bias_done[e - 2])
            tok_b2 = b.dma("g", b2e_flat[0:1, sl * D:(sl + 1) * D],
                           b2_h[e].rearrange("(a d) -> a d", a=1), dsmall, waits=b2w)
            for g in range(NG):
                gelu_toks = {}
                pb_free = [None, None]
                for m in range(FT):
                    pbk = m % 2
                    mmw = [tok_w1, moein_done, pb_free[pbk]]
                    if m < 2 and prev_w2_evs is not None:
                        mmw.append(prev_w2_evs[pbk])
                    tmm = None
                    for k in range(DT):
                        def mm(E, k=k, m=m, g=g, pbk=pbk):
                            wsrc = w1a if k < 4 else w1b
                            kk = k % 4
                            return E.matmul(bank(pbk),
                                            wsrc[:, kk * F + m * 128: kk * F + (m + 1) * 128],
                                            moeinT[:, k * S + g * 512: k * S + (g + 1) * 512],
                                            start=(k == 0), stop=(k == DT - 1))
                        tmm = b.op("t", mm, mmw if k == 0 else [], inc=sT if k == DT - 1 else None)
                    gt = b.op("s", lambda E, m=m, pbk=pbk, sl=sl:
                              E.activation(h1[:, m * 512:(m + 1) * 512], bank(pbk),
                                           AF.Gelu_apprx_tanh,
                                           bias=b1e[:, sl * FT + m: sl * FT + m + 1], scale=1.0),
                              [tmm, tok_b1], inc=sS)
                    pb_free[pbk] = gt
                    gelu_toks[m] = gt
                def w2_dma(j):
                    par = j % 2
                    w_ = [cast_toks[f"w2_{e}"], (dw2[par], dw2[par].n)]
                    if w2slot_free is not None:
                        w_.append(w2slot_free[j] if isinstance(w2slot_free, dict) else w2slot_free)
                    if j >= 2:
                        w_.append(chunk_done[j - 2])
                    return b.dma("q",
                                 w2win[:].rearrange("p (j2 d) -> p j2 d", j2=8)[:, par * 4:(par + 1) * 4, :],
                                 w2_s[e].rearrange("(kt p) d -> p kt d", p=128)[:, j * 4:(j + 1) * 4, :],
                                 dw2[par], waits=w_)
                chunk_done = {}
                w2ch = {}
                w2ch[0] = w2_dma(0)
                w2ch[1] = w2_dma(1)
                tmm = None
                for j in range(8):
                    par = j % 2
                    for k4 in range(4):
                        k = j * 4 + k4
                        for tl in range(4):
                            for ncc in range(2):
                                bi = tl * 2 + ncc
                                mmw = []
                                if tl == 0 and ncc == 0:
                                    mmw = [w2ch[j], gelu_toks[k]]
                                    if k == 0:
                                        mmw.append(gelu_toks[FT - 1])
                                if k == 0 and prev_w2_evs is not None:
                                    mmw.append(prev_w2_evs[bi])
                                def mm(E, k=k, par=par, k4=k4, tl=tl, ncc=ncc, bi=bi):
                                    return E.matmul(bank(bi),
                                                    h1[:, k * 512 + tl * 128: k * 512 + (tl + 1) * 128],
                                                    w2win[:, (par * 4 + k4) * D + ncc * 512:
                                                          (par * 4 + k4) * D + (ncc + 1) * 512],
                                                    start=(k == 0), stop=False)
                                last = (k4 == 3 and tl == 3 and ncc == 1)
                                tmm = b.op("t", mm, mmw, inc=sT if last else None)
                    chunk_done[j] = tmm
                    if j + 2 < 8:
                        w2ch[j + 2] = w2_dma(j + 2)
                bias_mm = None
                for bi in range(8):
                    ncc = bi % 2
                    def mm(E, bi=bi, ncc=ncc, sl=sl):
                        return E.matmul(bank(bi), ones_col[0:1, :],
                                        b2e_flat[0:1, sl * D + ncc * 512: sl * D + (ncc + 1) * 512],
                                        start=False, stop=True)
                    bias_mm = b.op("t", mm, [tok_b2] if bi == 0 else [], inc=sT)
                w2slot_free = {j: chunk_done[6 + (j % 2)] for j in range(8)}
                bias_done[e] = bias_mm
                evs = []
                for bi in range(8):
                    tl, ncc = bi // 2, bi % 2
                    t_glob = g * 4 + tl
                    def ev(E, bi=bi, ncc=ncc, t_glob=t_glob, e=e):
                        gcol = gate_w[:, t_glob * NEXP + e: t_glob * NEXP + e + 1]
                        dst = acc[:, t_glob * D + ncc * 512: t_glob * D + (ncc + 1) * 512]
                        return E.scalar_tensor_tensor(dst, bank(bi), gcol, dst, ALU.mult, ALU.add)
                    evt = b.op("v", ev, [bias_mm, gate_toks[t_glob], last_acc[t_glob]], inc=sV)
                    last_acc[t_glob] = evt
                    evs.append(evt)
                prev_w2_evs = evs
                if g == NG - 1:
                    w1_use_done = gelu_toks[FT - 1]
        return last_acc

    def phase_fin(bb, last_acc):
        all_moe = [last_acc[t] for t in range(TT)]
        prev_store = {}
        last_store = None
        for c in range(4):
            par = c % 2
            w_ = list(all_moe)
            if c >= 2 and prev_store.get(c - 2) is not None:
                w_.append(prev_store[c - 2])
            tok_ld = b.dma("q", fxstage[par][:].rearrange("p (tt d) -> p tt d", tt=4),
                           x_h[bb, c * 512:(c + 1) * 512, :].rearrange("(tt p) d -> p tt d", p=128),
                           dx[par], waits=w_)
            add_toks = []
            for tc in range(4):
                t = c * 4 + tc
                def ad(E, par=par, tc=tc, t=t):
                    return E.tensor_tensor(ostage[par][:, tc * D:(tc + 1) * D],
                                           fxstage[par][:, tc * D:(tc + 1) * D],
                                           acc[:, t * D:(t + 1) * D], ALU.add)
                w2_ = [tok_ld, last_acc[t]]
                if c >= 2 and prev_store.get(c - 2) is not None:
                    w2_.append(prev_store[c - 2])
                add_toks.append(b.op("v", ad, w2_, inc=sV))
            st_ = b.dma("g", y_h[bb, c * 512:(c + 1) * 512, :].rearrange("(tt p) d -> p tt d", p=128),
                        ostage[par][:].rearrange("p (tt d) -> p tt d", tt=4),
                        dout, waits=add_toks)
            prev_store[c] = st_
            last_store = st_
        return last_store

    batch_done = None
    for bb in range(nbatch):
        p0_done = phase_p0(bb, batch_done) if "p0" in stages else None
        if bb == 0:
            record_big_casts(p0_done)
        if p0_done is not None:
            tap(f"xnT{bb}", xnT[:], [p0_done])
            tap(f"hT{bb}", hT[:], [p0_done])
        pb2_done = phase_pb2(bb, p0_done) if "pb2" in stages else None
        if pb2_done is not None:
            tap(f"oT{bb}", oT[:], [pb2_done])
        pb3_done = phase_pb3(bb, pb2_done, p0_done) if "pb3" in stages else None
        if pb3_done is not None:
            tap(f"xresD{bb}", xres_d, [pb3_done])
        pa_done = phase_pa(bb, pb3_done, p0_done) if "pa" in stages else None
        if pa_done is not None:
            tap(f"awoD{bb}", awo_d, [pa_done])
        if "asm" in stages:
            moein_done, gate_toks, acc_toks, asm_done = phase_asm(bb, pa_done, pb3_done)
            tap(f"moeinT{bb}", moeinT[:], [moein_done, asm_done])
            tap(f"gate{bb}", gate_w[:], [asm_done])
            batch_done = asm_done
        if "moe" in stages:
            last_acc = phase_moe(bb, moein_done, gate_toks, acc_toks, asm_done)
            tap(f"accT{bb}", acc[:], [last_acc[TT - 1]])
            batch_done = last_acc[TT - 1]
        if "fin" in stages:
            batch_done = phase_fin(bb, last_acc)

    final_waits = [(dout, dout.n)]
    if batch_done is not None:
        final_waits.append(batch_done)
    b.op("g", lambda E: E.engine_nop(), final_waits)

    with nc.Block() as block:
        b.finish(block)
    es.close()
    return nc, tap_names


NAME_MAP = {
    "x": "L0_attn_x", "Wq": "L0_attn_Wq", "bq": "L0_attn_bq",
    "Wk": "L0_attn_Wk", "bk": "L0_attn_bk", "Wv": "L0_attn_Wv",
    "bv": "L0_attn_bv", "Wo": "L0_attn_Wo", "bo": "L0_attn_bo",
    "ln1_g": "L0_ffn_ln1_g", "ln1_b": "L0_ffn_ln1_b",
    "ln2_g": "L0_ffn_ln2_g", "ln2_b": "L0_ffn_ln2_b",
    "w_qkv": "L0_ffn_w_qkv", "w_o": "L0_ffn_w_o", "w_gate": "L0_ffn_w_gate",
    "w1": "L0_ffn_w1", "b1": "L0_ffn_b1", "w2": "L0_ffn_w2", "b2": "L0_ffn_b2",
}


def make_in_maps(inputs, n_cores=8):
    x = np.ascontiguousarray(np.asarray(inputs[NAME_MAP["x"]], dtype=np.float32))
    base = {k: np.ascontiguousarray(np.asarray(inputs[v], dtype=np.float32))
            for k, v in NAME_MAP.items() if k != "x"}
    in_maps = []
    for c in range(n_cores):
        m = dict(base)
        m["x"] = x[c * B_LOC:(c + 1) * B_LOC]
        in_maps.append(m)
    return in_maps


def kernel(**inputs):
    nc, _ = build()
    in_maps = make_in_maps(inputs)
    res = run_bass_kernel_spmd(nc, in_maps, core_ids=list(range(8)))
    return np.concatenate([r["y"] for r in res.results], axis=0)


# version 07 (final)
# speedup vs baseline: 1.1909x; 1.1909x over previous
import contextlib
import numpy as np
import ml_dtypes

import concourse.bass as bass
import concourse.mybir as mybir
from concourse.bass_utils import run_bass_kernel_spmd

F32 = mybir.dt.float32
BF16 = mybir.dt.bfloat16
AF = mybir.ActivationFunctionType
ALU = mybir.AluOpType
AX = mybir.AxisListType

B_LOC = 2
S = 2048
D = 1024
DT = D // 128
TT = S // 128
H = 16
HP = H // 2
F = 4096
FT = F // 128
NEXP = 8
GRP = 512
NG = S // GRP
NEG = -1.0e30
EPS = 1e-5

STAGES = ("p0", "pb2", "pb3", "pa", "asm", "moe", "fin")
KB = 1024


SB_BASE = 16512

def B(i, off=0):
    return SB_BASE + 24 * KB + i * 32 * KB + off

def SM(off=0):
    return SB_BASE + 24 * KB + 5 * 32 * KB + off


class Sem:
    __slots__ = ("h", "n", "name")


class Bld:
    ENGS = ("t", "v", "s", "g", "q")

    def __init__(self, nc, es):
        self.nc = nc
        self.es = es
        self.ops = {e: [] for e in self.ENGS}

    def sem(self, name):
        s = Sem()
        s.h = self.es.enter_context(self.nc.semaphore(name))
        s.n = 0
        s.name = name
        return s

    def op(self, eng, fn, waits=(), inc=None, k=1):
        tok = None
        if inc is not None:
            inc.n += k
            tok = (inc, inc.n)
        self.ops[eng].append((tuple(w for w in waits if w is not None), fn,
                              (inc, k) if inc is not None else None))
        return tok

    def dma(self, eng, out, in_, sem, waits=(), **kw):
        def fn(EE, out=out, in_=in_, kw=kw):
            return EE.dma_start(out=out, in_=in_, **kw)
        return self.op(eng, fn, waits, inc=sem, k=16)

    def finish(self, block):
        decos = {"t": block.tensor, "v": block.vector, "s": block.scalar,
                 "g": block.gpsimd, "q": block.sync}
        for e in self.ENGS:
            opl = self.ops[e]
            def make(opl):
                def body(EE):
                    waited = {}
                    for waits, fn, inc in opl:
                        for (sm, val) in waits:
                            if waited.get(id(sm), -1) < val:
                                EE.wait_ge(sm.h, val)
                                waited[id(sm)] = val
                        inst = fn(EE)
                        if inc is not None:
                            inst.then_inc(inc[0].h, inc[1])
                return body
            decos[e](make(opl))


def cdiv(a, b):
    return (a + b - 1) // b


def build(stages=STAGES, taps=(), nbatch=B_LOC):
    nc = bass.Bass("TRN2", target_bir_lowering=False)
    es = contextlib.ExitStack()
    b = Bld(nc, es)

    x_h = nc.dram_tensor("x", [B_LOC, S, D], F32, kind="ExternalInput").ap()
    y_h = nc.dram_tensor("y", [B_LOC, S, D], F32, kind="ExternalOutput").ap()
    wnames_2d = ["Wq", "Wk", "Wv", "Wo", "w_o"]
    W_h = {n: nc.dram_tensor(n, [D, D], F32, kind="ExternalInput").ap() for n in wnames_2d}
    wqkv_h = nc.dram_tensor("w_qkv", [D, 3 * D], F32, kind="ExternalInput").ap()
    vec_h = {n: nc.dram_tensor(n, [D], F32, kind="ExternalInput").ap()
             for n in ["bq", "bk", "bv", "bo", "ln1_g", "ln1_b", "ln2_g", "ln2_b"]}
    wgate_h = nc.dram_tensor("w_gate", [D, NEXP], F32, kind="ExternalInput").ap()
    w1_h = nc.dram_tensor("w1", [NEXP, D, F], F32, kind="ExternalInput").ap()
    b1_h = nc.dram_tensor("b1", [NEXP, F], F32, kind="ExternalInput").ap()
    w2_h = nc.dram_tensor("w2", [NEXP, F, D], F32, kind="ExternalInput").ap()
    b2_h = nc.dram_tensor("b2", [NEXP, D], F32, kind="ExternalInput").ap()

    Ws = {n: nc.dram_tensor(n + "_s", [D, D], BF16, kind="Internal").ap() for n in wnames_2d}
    wqkv_s = nc.dram_tensor("wqkv_s", [D, 3 * D], BF16, kind="Internal").ap()
    w1_s = nc.dram_tensor("w1_s", [NEXP, D, F], BF16, kind="Internal").ap()
    w2_s = nc.dram_tensor("w2_s", [NEXP, F, D], BF16, kind="Internal").ap()
    xres_d = nc.dram_tensor("xres_d", [D, S], BF16, kind="Internal").ap()
    awo_d = nc.dram_tensor("awo_d", [D, S], BF16, kind="Internal").ap()

    eye_bf_h = nc.inline_tensor(np.eye(128, dtype=ml_dtypes.bfloat16), name="c_eye_bf").ap()
    eye_f_h = nc.inline_tensor(np.eye(128, dtype=np.float32), name="c_eye_f").ap()
    cmask_np = np.where(np.tril(np.ones((128, 128))) > 0, 0.0, NEG).astype(np.float32)
    cmask_h = nc.inline_tensor(cmask_np, name="c_cmask").ap()
    ones_row_h = nc.inline_tensor(np.ones((1, 512), dtype=ml_dtypes.bfloat16), name="c_ones_row").ap()
    ones_col_h = nc.inline_tensor(np.ones((1, 128), dtype=ml_dtypes.bfloat16), name="c_ones_col").ap()
    ones_bigcol_h = nc.inline_tensor(np.ones((128, 1), dtype=ml_dtypes.bfloat16), name="c_ones_bigcol").ap()
    eps_col_h = nc.inline_tensor(np.full((128, 1), EPS, dtype=np.float32), name="c_eps_col").ap()

    def sb(name, shape, dtype, off):
        return nc.alloc_sbuf_tensor_at(name, shape, dtype, offset=off)

    p = SB_BASE
    eye_bf = sb("eye_bf", [128, 128], BF16, p); p += 256
    eye_f = sb("eye_f", [128, 128], F32, p); p += 512
    cmask = sb("cmask", [128, 128], F32, p); p += 512
    ones_row = sb("ones_row", [1, 512], BF16, p); p += 1024
    ones_col = sb("ones_col", [1, 128], BF16, p); p += 256
    ones_bigcol = sb("ones_bigcol", [128, 1], BF16, p); p += 32
    eps_col = sb("eps_col", [128, 1], F32, p); p += 32
    vecs = sb("vecs", [128, 64], F32, p); p += 256
    VC = {"bq": 0, "bk": 8, "bo": 16, "ln1_g": 24, "ln1_b": 32, "ln2_g": 40, "ln2_b": 48}
    wg_t = sb("wg_t", [128, DT * NEXP], F32, p); p += 256
    bv_flat = sb("bv_flat", [1, D], BF16, p); p += 2048
    b2e_flat = sb("b2e_flat", [1, 2 * D], BF16, p); p += 4096
    b1e = sb("b1e", [128, 2 * FT], F32, p); p += 256
    qsqh = sb("qsqh", [1, S], BF16, p); p += 4096
    ksqh = sb("ksqh", [1, S], BF16, p); p += 4096
    stats = sb("stats", [128, 64], F32, p); p += 256
    gl_tok = sb("gl_tok", [128, TT * NEXP], F32, p); p += 512
    gate_w = sb("gate_w", [128, TT * NEXP], F32, p); p += 512
    gjunk = sb("gjunk", [128, 24], F32, p); p += 96
    assert p <= SB_BASE + 24 * KB, p

    xnT = sb("xnT", [128, DT * S], BF16, B(0))
    hT = sb("hT", [128, DT * S], BF16, B(1))
    xstage = [sb("xstage0", [128, 4 * D], F32, B(2)),
              sb("xstage1", [128, 4 * D], F32, B(3))]
    p0cx = sb("p0cx", [128, D], F32, SM(0))
    p0junk = sb("p0junk", [128, D], F32, SM(4 * KB))
    p0xn = sb("p0xn", [128, D], BF16, SM(8 * KB))
    p0hn = sb("p0hn", [128, D], BF16, SM(10 * KB))
    oT = sb("oT", [128, DT * S], BF16, B(2))
    qpT = sb("qpT", [128, S], BF16, SM(0))
    kpT = sb("kpT", [128, S], BF16, SM(4 * KB))
    vp = sb("vp", [128, TT * 128], BF16, SM(8 * KB))
    wqkvp = sb("wqkvp", [128, DT * 384], BF16, SM(12 * KB))
    a_bfi = [sb("a_bfi0", [128, S], BF16, B(3)),
             sb("a_bfi1", [128, S], BF16, B(3, 4 * KB)),
             sb("a_bfi2", [128, S], BF16, B(3, 8 * KB)),
             sb("a_bfi3", [128, S], BF16, B(3, 12 * KB))]
    aT = sb("aT", [128, TT * 512], BF16, B(3, 16 * KB))
    wbuf = sb("wbuf", [128, DT * D], BF16, SM(0))
    xres_stage = sb("xres_stage", [128, DT * 512], BF16, B(4))
    QT = sb("QT", [128, DT * S], BF16, B(2))
    KT_ = sb("KT", [128, DT * S], BF16, B(4))
    VA = sb("VA", [128, TT * D], BF16, B(1))
    s_scr2 = [sb("s_scr0", [128, S], F32, B(3)),
              sb("s_scr1", [128, S], F32, B(3, 8 * KB))]
    oa_chunk = sb("oa_chunk", [128, DT * 512], BF16, B(0))
    awo_stage = sb("awo_stage", [128, DT * 512], BF16, B(0, 8 * KB))
    a_bfa = [sb("a_bfa0", [128, S], BF16, B(0, 16 * KB)),
             sb("a_bfa1", [128, S], BF16, B(0, 20 * KB)),
             sb("a_bfa2", [128, S], BF16, B(0, 24 * KB)),
             sb("a_bfa3", [128, S], BF16, B(0, 28 * KB))]
    moechunk = sb("moechunk", [128, DT * 512], F32, B(0))
    c_f = sb("c_f", [128, D], F32, B(0, 16 * KB))
    nrm_f = sb("nrm_f", [128, D], F32, B(0, 20 * KB))
    jnk_f = sb("jnk_f", [128, D], F32, B(0, 24 * KB))
    glT = sb("glT", [8, S], F32, B(2))
    xres_tok = sb("xres_tok", [128, 2 * D], BF16, B(2, 8 * KB))
    awo_tok = sb("awo_tok", [128, 2 * D], BF16, B(2, 12 * KB))
    moeinT = sb("moeinT", [128, DT * S], BF16, B(3))
    acc = sb("acc", [128, TT * D], BF16, B(1))
    h1 = sb("h1", [128, FT * GRP], BF16, B(0))
    w1a = sb("w1a", [128, 4 * F], BF16, B(2))
    w1b = sb("w1b", [128, 4 * F], BF16, B(4))
    w2win = sb("w2win", [128, 8 * D], BF16, SM(0))
    fxstage = [sb("fxstage0", [128, 4 * D], F32, B(2)),
               sb("fxstage1", [128, 4 * D], F32, B(2, 16 * KB))]
    ostage = [sb("ostage0", [128, 4 * D], F32, B(4)),
              sb("ostage1", [128, 4 * D], F32, B(4, 16 * KB))]

    PSA = es.enter_context(nc.psum_tensor("PSA", [128, 8 * 512], F32))

    def bank(i):
        return PSA[:, i * 512:(i + 1) * 512]

    tr_bf = [PSA[:, 6 * 512:7 * 512].bitcast(BF16)[:, 0:512],
             PSA[:, 7 * 512:8 * 512].bitcast(BF16)[:, 0:512]]

    sT = b.sem("sT"); sV = b.sem("sV"); sS = b.sem("sS")
    dcast = b.sem("dcast"); dconst = b.sem("dconst")
    dx = [b.sem("dx0"), b.sem("dx1")]
    dwb = b.sem("dwb"); dwp = b.sem("dwp"); dw1 = b.sem("dw1")
    dw2 = [b.sem("dw2a"), b.sem("dw2b")]
    dsmall = b.sem("dsmall"); dout = b.sem("dout")
    dsp = b.sem("dsp")
    dxt = [b.sem("dxt0"), b.sem("dxt1")]

    b.dma("q", eye_bf[:], eye_bf_h, dconst)
    b.dma("q", eye_f[:], eye_f_h, dconst)
    b.dma("q", cmask[:], cmask_h, dconst)
    b.dma("q", ones_row[:], ones_row_h, dconst)
    b.dma("q", ones_col[:], ones_col_h, dconst)
    b.dma("q", ones_bigcol[:], ones_bigcol_h, dconst)
    b.dma("q", eps_col[:], eps_col_h, dconst)
    for n, c in VC.items():
        b.dma("q", vecs[:, c:c + 8], vec_h[n].rearrange("(k p) -> p k", p=128), dconst,
              allow_slow_non_contiguous=True)
    b.dma("q", wg_t[:].rearrange("p (k e) -> p k e", k=DT),
          wgate_h.rearrange("(k p) e -> p k e", p=128), dconst)
    tok_const = (dconst, dconst.n)

    cast_toks = {}

    def cast(name, dst, src):
        t = b.dma("g", dst, src, dcast, waits=[(dcast, dcast.n)])
        cast_toks[name] = t
        return t

    cast("bv", bv_flat[0:1, :], vec_h["bv"].rearrange("(a d) -> a d", a=1))
    cast("wqkv", wqkv_s, wqkv_h)
    cast("w_o", Ws["w_o"], W_h["w_o"])
    for n in ["Wk", "Wv", "Wq", "Wo"]:
        cast(n, Ws[n], W_h[n])
    big_casts = []
    def record_big_casts(gate_tok):
        first = True
        for e in range(NEXP):
            for nm_, dst, src_ in ((f"w1_{e}", w1_s[e], w1_h[e]),
                                   (f"w2_{e}", w2_s[e], w2_h[e])):
                w_ = [(dcast, dcast.n)]
                if first and gate_tok is not None:
                    w_.append(gate_tok)
                    first = False
                cast_toks[nm_] = b.dma("g", dst, src_, dcast, waits=w_)

    tap_names = []

    def tap(name, ap_, waits):
        if name in taps:
            th = nc.dram_tensor("tap_" + name, list(ap_.shape), ap_.dtype,
                                kind="ExternalOutput").ap()
            b.dma("g", th, ap_, dout, waits=list(waits))
            tap_names.append("tap_" + name)

    tr_free = [None, None]

    def phase_p0(bb, start_tok):
        last_use = [None, None]
        vchain = start_tok
        for c in range(4):
            par = c % 2
            w = [tok_const, start_tok]
            if last_use[par] is not None:
                w.append(last_use[par])
            tok_ld = b.dma("q", xstage[par][:].rearrange("p (tt d) -> p tt d", tt=4),
                           x_h[bb, c * 512:(c + 1) * 512, :].rearrange("(tt p) d -> p tt d", p=128),
                           dx[par], waits=w)
            for tc in range(4):
                t = c * 4 + tc
                xst = xstage[par][:, tc * D:(tc + 1) * D]
                sp = (t % 2) * 8
                sumx = stats[:, sp:sp + 1]; negm = stats[:, sp + 1:sp + 2]
                vsum = stats[:, sp + 2:sp + 3]; rstd = stats[:, sp + 3:sp + 4]
                t1 = b.op("v", lambda E, o=sumx, i=xst: E.reduce_sum(o, i, axis=AX.X),
                          [tok_ld, vchain], inc=sV)
                t2 = b.op("v", lambda E, o=negm, i=sumx: E.tensor_scalar(o, i, -1.0 / D, None, ALU.mult),
                          [t1], inc=sV)
                t3 = b.op("v", lambda E, o=p0cx[:], i=xst, s1=negm: E.tensor_scalar(o, i, s1, None, ALU.add),
                          [t2], inc=sV)
                t4 = b.op("v", lambda E, o=p0junk[:], i=p0cx[:], a=vsum:
                          E.scalar_tensor_tensor(o, i, 1.0, i, ALU.mult, ALU.mult, accum_out=a),
                          [t3], inc=sV)
                t5 = b.op("s", lambda E, o=rstd, i=vsum: E.activation(o, i, AF.Ln, bias=eps_col[:], scale=1.0 / D),
                          [t4], inc=sS)
                t6 = b.op("s", lambda E, o=rstd, i=rstd: E.activation(o, i, AF.Exp, bias=0.0, scale=-0.5),
                          [t5], inc=sS)
                t7 = b.op("v", lambda E, o=p0xn[:], i=p0cx[:], s1=rstd: E.tensor_scalar(o, i, s1, None, ALU.mult),
                          [t6], inc=sV)
                u1 = b.op("v", lambda E, o=sumx, i=p0xn[:]: E.reduce_sum(o, i, axis=AX.X), [t7], inc=sV)
                u2 = b.op("v", lambda E, o=negm, i=sumx: E.tensor_scalar(o, i, -1.0 / D, None, ALU.mult),
                          [u1], inc=sV)
                u3 = b.op("v", lambda E, o=p0cx[:], i=p0xn[:], s1=negm: E.tensor_scalar(o, i, s1, None, ALU.add),
                          [u2], inc=sV)
                u4 = b.op("v", lambda E, o=p0junk[:], i=p0cx[:], a=vsum:
                          E.scalar_tensor_tensor(o, i, 1.0, i, ALU.mult, ALU.mult, accum_out=a),
                          [u3], inc=sV)
                u5 = b.op("s", lambda E, o=rstd, i=vsum: E.activation(o, i, AF.Ln, bias=eps_col[:], scale=1.0 / D),
                          [u4], inc=sS)
                u6 = b.op("s", lambda E, o=rstd, i=rstd: E.activation(o, i, AF.Exp, bias=0.0, scale=-0.5),
                          [u5], inc=sS)
                u7 = b.op("v", lambda E, o=p0hn[:], i=p0cx[:], s1=rstd: E.tensor_scalar(o, i, s1, None, ALU.mult),
                          [u6], inc=sV)
                last_use[par] = t3
                vchain = u7
                for half, srcbuf in [(0, p0xn), (1, p0hn)]:
                    dep = t7 if half == 0 else u7
                    for g4 in range(2):
                        tri = (t * 4 + half * 2 + g4) % 2
                        trt = []
                        for j in range(4):
                            kk = g4 * 4 + j
                            w_ = [dep] if j == 0 else []
                            if j == 0 and tr_free[tri] is not None:
                                w_.append(tr_free[tri])
                            trt.append(b.op("t", lambda E, kk=kk, j=j, tri=tri, srcbuf=srcbuf:
                                            E.transpose(tr_bf[tri][:, j * 128:(j + 1) * 128],
                                                        srcbuf[:, kk * 128:(kk + 1) * 128], eye_bf[:]),
                                            w_, inc=sT))
                        if half == 0:
                            def ev(E, tri=tri, g4=g4, t=t):
                                dst = xnT[:].rearrange("p (k s) -> p k s", k=DT)[:, g4 * 4:g4 * 4 + 4,
                                                                                t * 128:(t + 1) * 128]
                                src = tr_bf[tri][:, 0:512].rearrange("p (j c) -> p j c", j=4)
                                return E.tensor_copy(dst, src)
                            evt = b.op("v", ev, [trt[-1], vchain], inc=sV)
                            tr_free[tri] = evt
                            vchain = evt
                        else:
                            evt = None
                            for j in range(4):
                                kk = g4 * 4 + j
                                def ev1(E, tri=tri, j=j, kk=kk, t=t):
                                    return E.tensor_scalar(
                                        hT[:, kk * S + t * 128: kk * S + (t + 1) * 128],
                                        tr_bf[tri][:, j * 128:(j + 1) * 128],
                                        vecs[:, VC["ln1_g"] + kk:VC["ln1_g"] + kk + 1],
                                        vecs[:, VC["ln1_b"] + kk:VC["ln1_b"] + kk + 1],
                                        ALU.mult, ALU.add)
                                w_ = [trt[-1], vchain] if evt is None else [trt[-1], evt]
                                evt = b.op("v", ev1, w_, inc=sV)
                            tr_free[tri] = evt
                            vchain = evt
        return vchain

    def phase_pb2(bb, p0_done):
        prev_pair_done = None
        sbank = [None, None, None, None]
        statg = [None, None, None, None]
        abf_free = [None, None, None, None]
        hp_count = 1 if "pb2short" in stages else HP
        for hp in range(hp_count):
            wts = [cast_toks["wqkv"], (dwp, dwp.n)]
            if prev_pair_done is not None:
                wts.append(prev_pair_done)
            tok_w = None
            for part in range(3):
                col0 = part * D + hp * 128
                tok_w = b.dma("q",
                              wqkvp[:].rearrange("p (k c) -> p k c", k=DT)[:, :, part * 128:(part + 1) * 128],
                              wqkv_s.rearrange("(k p) n -> p k n", p=128)[:, :, col0:col0 + 128],
                              dwp, waits=wts if part == 0 else [])
            pb_free = [None, None]
            prod = {}
            for which, dstbuf in [(0, qpT), (1, kpT)]:
                for nck in range(4):
                    pbk = (which * 4 + nck) % 2
                    mmw = [tok_w, p0_done]
                    if pb_free[pbk] is not None:
                        mmw.append(pb_free[pbk])
                    if prev_pair_done is not None:
                        mmw.append(prev_pair_done)
                    tmm = None
                    for k in range(DT):
                        def mm(E, k=k, which=which, nck=nck, pbk=pbk):
                            return E.matmul(bank(pbk),
                                            wqkvp[:, k * 384 + which * 128: k * 384 + (which + 1) * 128],
                                            hT[:, k * S + nck * 512: k * S + (nck + 1) * 512],
                                            start=(k == 0), stop=(k == DT - 1))
                        tmm = b.op("t", mm, mmw if k == 0 else [], inc=sT if k == DT - 1 else None)
                    ev = b.op("v", lambda E, d=dstbuf[:, nck * 512:(nck + 1) * 512], pbk=pbk:
                              E.tensor_copy(d, bank(pbk)), [tmm], inc=sV)
                    pb_free[pbk] = ev
                    prod[(which, nck)] = ev
            vtok = None
            for mt in range(TT):
                pbk = mt % 2
                mmw = [tok_w, p0_done]
                if pb_free[pbk] is not None:
                    mmw.append(pb_free[pbk])
                tmm = None
                for k in range(DT):
                    def mm(E, k=k, mt=mt, pbk=pbk):
                        return E.matmul(bank(pbk)[:, 0:128],
                                        hT[:, k * S + mt * 128: k * S + (mt + 1) * 128],
                                        wqkvp[:, k * 384 + 256: k * 384 + 384],
                                        start=(k == 0), stop=(k == DT - 1))
                    tmm = b.op("t", mm, mmw if k == 0 else [], inc=sT if k == DT - 1 else None)
                ev = b.op("v", lambda E, mt=mt, pbk=pbk:
                          E.tensor_copy(vp[:, mt * 128:(mt + 1) * 128], bank(pbk)[:, 0:128]),
                          [tmm], inc=sV)
                pb_free[pbk] = ev
                vtok = ev
            qkv_done = [prod[(0, 3)], prod[(1, 3)], vtok]
            o_last = None
            abf_free = [None, None, None, None]
            sbank = [None, None, None, None]
            statg = [None, None, None, None]
            srot = 0
            prev_o = prev_pair_done
            for hh in range(2):
                pr = 64 * hh
                for qc in range(4):
                    ktc = 4 * qc + 4
                    atr_evs = []
                    for q4 in range(4):
                        qi = qc * 4 + q4
                        for kt in range(qi + 1, ktc):
                            z = b.op("v", lambda E, kt=kt, q4=q4:
                                     E.memset(aT[:, kt * 512 + q4 * 128: kt * 512 + (q4 + 1) * 128], 0.0),
                                     [prev_o], inc=sV)
                            atr_evs.append(z)
                    for q4 in range(4):
                        qi = qc * 4 + q4
                        wkt = qi + 1
                        wid = wkt * 128
                        nch = cdiv(wid, 512)
                        par = q4 % 4
                        sb_ = 16 + par * 12
                        maxc = [stats[:, sb_ + i:sb_ + i + 1] for i in range(4)]
                        gmax = stats[:, sb_ + 4:sb_ + 5]
                        negms = stats[:, sb_ + 5:sb_ + 6]
                        denc = [stats[:, sb_ + 6 + i:sb_ + 7 + i] for i in range(4)]
                        dens = stats[:, sb_ + 10:sb_ + 11]
                        rr = stats[:, sb_ + 11:sb_ + 12]
                        dgb = (qi * 128) // 512
                        bk = {ci: 2 + (srot + ci) % 4 for ci in range(nch)}
                        mm_ci = {}
                        for ci in range(nch):
                            nw = min(512, wid - ci * 512)
                            def mm(E, ci=ci, nw=nw, qi=qi, pr=pr, bb_=bk[ci]):
                                return E.matmul(PSA[:, bb_ * 512:bb_ * 512 + nw],
                                                qpT[pr:pr + 64, qi * 128:(qi + 1) * 128],
                                                kpT[pr:pr + 64, ci * 512:ci * 512 + nw],
                                                start=True, stop=True)
                            w_ = [sbank[bk[ci] - 2]] + (qkv_done if (ci == 0 and qc == 0 and hh == 0) else [])
                            mm_ci[ci] = b.op("t", mm, w_, inc=sT)
                        for _w in range(24):
                            b.op("t", lambda E: E.ldweights(eye_bf[:]), [])
                        mt1 = b.op("v", lambda E, qi=qi, bb_=bk[dgb]:
                                   E.tensor_tensor(PSA[:, bb_ * 512 + (qi * 128) % 512:
                                                       bb_ * 512 + (qi * 128) % 512 + 128],
                                                   PSA[:, bb_ * 512 + (qi * 128) % 512:
                                                       bb_ * 512 + (qi * 128) % 512 + 128],
                                                   cmask[:], ALU.add),
                                   [mm_ci[dgb], statg[par]], inc=sV)
                        w_ = [mm_ci[0], statg[par]]
                        if dgb == 0:
                            w_.append(mt1)
                        nw0 = min(512, wid)
                        gm = b.op("v", lambda E, o=gmax, nw0=nw0, bb_=bk[0]:
                                  E.reduce_max(o, PSA[:, bb_ * 512:bb_ * 512 + nw0],
                                               axis=AX.X, negate=True), w_, inc=sV)
                        ng = b.op("v", lambda E, o=negms, i=gmax:
                                  E.tensor_scalar(o, i, 0.125, None, ALU.mult), [gm], inc=sV)
                        abuf = a_bfi[par]
                        e_toks = []
                        for ci in range(nch):
                            nw = min(512, wid - ci * 512)
                            w_ = [ng, mm_ci[ci]]
                            if ci == dgb:
                                w_.append(mt1)
                            if ci == 0:
                                w_.append(abf_free[par])
                            def ex(E, ci=ci, nw=nw, abuf=abuf, bias=negms, acc_=denc[ci], bb_=bk[ci]):
                                return E.activation(abuf[:, ci * 512:ci * 512 + nw],
                                                    PSA[:, bb_ * 512:bb_ * 512 + nw],
                                                    AF.Exp, bias=bias, scale=0.125, accum_out=acc_)
                            et = b.op("s", ex, w_, inc=sS)
                            e_toks.append(et)
                            sbank[bk[ci] - 2] = et
                        srot = (srot + nch) % 4
                        ds = b.op("v", lambda E, o=dens, nch=nch, sb_=sb_:
                                  E.reduce_sum(o, stats[:, sb_ + 6:sb_ + 6 + nch], axis=AX.X),
                                  e_toks, inc=sV)
                        rv = b.op("v", lambda E, o=rr, i=dens: E.reciprocal(o, i), [ds], inc=sV)
                        nm = b.op("v", lambda E, abuf=abuf, wid=wid, s1=rr:
                                  E.tensor_scalar(abuf[:, 0:wid], abuf[:, 0:wid], s1, None, ALU.mult),
                                  [rv], inc=sV)
                        statg[par] = nm
                        for g4 in range(cdiv(wkt, 4)):
                            nblk = min(4, wkt - g4 * 4)
                            tri = (qi + g4 + hh) % 2
                            trt = []
                            for j in range(nblk):
                                kb = g4 * 4 + j
                                w_ = [nm] if j == 0 else []
                                if j == 0 and tr_free[tri] is not None:
                                    w_.append(tr_free[tri])
                                trt.append(b.op("t", lambda E, kb=kb, j=j, tri=tri, abuf=abuf:
                                                E.transpose(tr_bf[tri][:, j * 128:(j + 1) * 128],
                                                            abuf[:, kb * 128:(kb + 1) * 128], eye_bf[:]),
                                                w_, inc=sT))
                            def ev(E, tri=tri, g4=g4, q4=q4, nblk=nblk):
                                dst = aT[:].rearrange("p (k c) -> p k c", k=TT)[:, g4 * 4:g4 * 4 + nblk,
                                                                               q4 * 128:(q4 + 1) * 128]
                                src = tr_bf[tri][:, 0:nblk * 128].rearrange("p (j c) -> p j c", j=nblk)
                                return E.tensor_copy(dst, src)
                            evt = b.op("v", ev, [trt[-1], prev_o], inc=sV)
                            tr_free[tri] = evt
                            atr_evs.append(evt)
                        abf_free[par] = atr_evs[-1]
                    pbk = (qc + hh) % 2
                    mmw = atr_evs + [pb_free[pbk]]
                    tmm = None
                    for kt in range(ktc):
                        if hh == 0:
                            def mm(E, kt=kt, pr=pr, pbk=pbk, ktc=ktc):
                                return E.matmul(PSA[0:64, pbk * 512:(pbk + 1) * 512],
                                                vp[:, kt * 128 + pr: kt * 128 + pr + 64],
                                                aT[:, kt * 512:(kt + 1) * 512],
                                                start=(kt == 0), stop=(kt == ktc - 1))
                        else:
                            def mm(E, kt=kt, pr=pr, pbk=pbk, ktc=ktc):
                                return E.matmul(PSA[64:128, pbk * 512:(pbk + 1) * 512],
                                                vp[:, kt * 128 + pr: kt * 128 + pr + 64],
                                                aT[:, kt * 512:(kt + 1) * 512],
                                                start=(kt == 0), stop=(kt == ktc - 1),
                                                tile_position=(0, 64))
                        tmm = b.op("t", mm, mmw if kt == 0 else [], inc=sT if kt == ktc - 1 else None)
                    prev_o = tmm
                    def ev(E, hp=hp, pr=pr, qc=qc, pbk=pbk, hh=hh):
                        src = PSA[0:64, pbk * 512:(pbk + 1) * 512] if hh == 0 else \
                              PSA[64:128, pbk * 512:(pbk + 1) * 512]
                        return E.tensor_copy(oT[pr:pr + 64, hp * S + qc * 512: hp * S + (qc + 1) * 512], src)
                    evt = b.op("v", ev, [tmm], inc=sV)
                    pb_free[pbk] = evt
                    o_last = evt
            prev_pair_done = o_last
        return prev_pair_done

    def phase_pb3(bb, pb2_done, p0_done):
        tok_w = b.dma("q", wbuf[:].rearrange("p (k n) -> p k n", k=DT),
                      Ws["w_o"].rearrange("(k p) n -> p k n", p=128), dwb,
                      waits=[cast_toks["w_o"], pb2_done, (dwb, dwb.n)])
        pb_free = [None, None]
        last_wr = None
        for ncc in range(4):
            ch_evs = []
            for m in range(DT):
                pbk = (ncc * DT + m) % 2
                mmw = [tok_w, pb2_done, p0_done, pb_free[pbk]]
                if m == 0 and last_wr is not None:
                    mmw.append(last_wr)
                tmm = None
                for k in range(DT):
                    def mm(E, k=k, m=m, ncc=ncc, pbk=pbk):
                        return E.matmul(bank(pbk),
                                        wbuf[:, k * D + m * 128: k * D + (m + 1) * 128],
                                        oT[:, k * S + ncc * 512: k * S + (ncc + 1) * 512],
                                        start=(k == 0), stop=(k == DT - 1))
                    tmm = b.op("t", mm, mmw if k == 0 else [], inc=sT if k == DT - 1 else None)
                def ev(E, m=m, ncc=ncc, pbk=pbk):
                    return E.tensor_tensor(xres_stage[:, m * 512:(m + 1) * 512],
                                           bank(pbk),
                                           xnT[:, m * S + ncc * 512: m * S + (ncc + 1) * 512], ALU.add)
                w_ = [tmm]
                if last_wr is not None:
                    w_.append(last_wr)
                evt = b.op("v", ev, w_, inc=sV)
                pb_free[pbk] = evt
                ch_evs.append(evt)
            last_wr = b.dma("q", xres_d.rearrange("(k p) s -> p k s", p=128)[:, :, ncc * 512:(ncc + 1) * 512],
                            xres_stage[:].rearrange("p (k c) -> p k c", k=DT),
                            dsp, waits=ch_evs + [(dsp, dsp.n)])
        return last_wr

    def phase_pa(bb, pb3_done, p0_done):
        projdone = {}
        prev = pb3_done
        pb_free = [None, None]
        for wn, kind in [("Wk", "K"), ("Wv", "V"), ("Wq", "Q")]:
            tok_w = b.dma("q", wbuf[:].rearrange("p (k n) -> p k n", k=DT),
                          Ws[wn].rearrange("(k p) n -> p k n", p=128), dwb,
                          waits=[cast_toks[wn], prev, (dwb, dwb.n)])
            last = None
            if kind in ("K", "Q"):
                dst = KT_ if kind == "K" else QT
                bc = VC["bk"] if kind == "K" else VC["bq"]
                for ncc in range(4):
                    for m in range(DT):
                        pbk = (ncc * DT + m) % 2
                        mmw = [tok_w, p0_done, pb_free[pbk], pb3_done]
                        tmm = None
                        for k in range(DT):
                            def mm(E, k=k, m=m, ncc=ncc, pbk=pbk):
                                return E.matmul(bank(pbk),
                                                wbuf[:, k * D + m * 128: k * D + (m + 1) * 128],
                                                xnT[:, k * S + ncc * 512: k * S + (ncc + 1) * 512],
                                                start=(k == 0), stop=(k == DT - 1))
                            tmm = b.op("t", mm, mmw if k == 0 else [], inc=sT if k == DT - 1 else None)
                        def ev(E, m=m, ncc=ncc, pbk=pbk, bc=bc, dst=dst):
                            return E.activation(dst[:, m * S + ncc * 512: m * S + (ncc + 1) * 512],
                                                bank(pbk), AF.Identity,
                                                bias=vecs[:, bc + m:bc + m + 1], scale=1.0)
                        evt = b.op("s", ev, [tmm], inc=sS)
                        pb_free[pbk] = evt
                        last = evt
            else:
                for mt in range(TT):
                    for ncc in range(2):
                        pbk = (mt * 2 + ncc) % 2
                        mmw = [tok_w, p0_done, pb_free[pbk], cast_toks["bv"], tok_const]
                        tmm = None
                        for k in range(DT):
                            def mm(E, k=k, mt=mt, ncc=ncc, pbk=pbk):
                                return E.matmul(bank(pbk),
                                                xnT[:, k * S + mt * 128: k * S + (mt + 1) * 128],
                                                wbuf[:, k * D + ncc * 512: k * D + (ncc + 1) * 512],
                                                start=(k == 0), stop=False)
                            tmm = b.op("t", mm, mmw if k == 0 else [], inc=None)
                        tmm = b.op("t", lambda E, ncc=ncc, pbk=pbk:
                                   E.matmul(bank(pbk), ones_col[0:1, :],
                                            bv_flat[0:1, ncc * 512:(ncc + 1) * 512],
                                            start=False, stop=True), [], inc=sT)
                        evt = b.op("v", lambda E, mt=mt, ncc=ncc, pbk=pbk:
                                   E.tensor_copy(VA[:, mt * D + ncc * 512: mt * D + (ncc + 1) * 512],
                                                 bank(pbk)), [tmm], inc=sV)
                        pb_free[pbk] = evt
                        last = evt
            projdone[kind] = last
            prev = last

        sq_ev = {}
        for which, (srcb, dsth, pd) in enumerate([(QT, qsqh, "Q"), (KT_, ksqh, "K")]):
            prev_mm = None
            evt = None
            for k in range(DT):
                tqw = [projdone[pd], prev_mm, projdone["V"]]
                tq = b.op("v", lambda E, k=k, srcb=srcb:
                          E.tensor_tensor(s_scr2[0][:].bitcast(BF16)[:, 0:S],
                                          srcb[:, k * S:(k + 1) * S],
                                          srcb[:, k * S:(k + 1) * S], ALU.mult),
                          tqw, inc=sV)
                tmm = None
                for ncc in range(4):
                    def mm(E, ncc=ncc, k=k):
                        return E.matmul(PSA[0:1, (2 + ncc) * 512:(2 + ncc) * 512 + 512],
                                        ones_bigcol[:, 0:1],
                                        s_scr2[0][:].bitcast(BF16)[:, ncc * 512:(ncc + 1) * 512],
                                        start=(k == 0), stop=(k == DT - 1))
                    tmm = b.op("t", mm, [tq] if ncc == 0 else [], inc=sT if ncc == 3 else None)
                prev_mm = tmm
            for ncc in range(4):
                evt = b.op("v", lambda E, ncc=ncc, dsth=dsth:
                           E.tensor_scalar(dsth[0:1, ncc * 512:(ncc + 1) * 512],
                                           PSA[0:1, (2 + ncc) * 512:(2 + ncc) * 512 + 512],
                                           -0.5, None, ALU.mult), [prev_mm], inc=sV)
            sq_ev[which] = evt
        sq_all = [sq_ev[0], sq_ev[1]]
        sq_mm_done = sq_ev[1]

        tok_wo = b.dma("q", wbuf[:].rearrange("p (k n) -> p k n", k=DT),
                       Ws["Wo"].rearrange("(k p) n -> p k n", p=128), dwb,
                       waits=[cast_toks["Wo"], projdone["Q"], projdone["V"], (dwb, dwb.n)])

        abf_free = [None, None, None, None]
        awo_last = None
        pb2f = [None, None]
        sbankA = [sq_mm_done, sq_mm_done, sq_mm_done, sq_mm_done]
        scr_free = [None, None]
        statg = [None, None, None, None]
        prev_o = pb3_done
        for qc in range(4):
            atr_evs_c = []
            for q4 in range(4):
                qi = qc * 4 + q4
                par = q4 % 4
                sb_ = 16 + par * 12
                negm = stats[:, sb_:sb_ + 1]
                den = stats[:, sb_ + 1:sb_ + 2]
                rr = stats[:, sb_ + 2:sb_ + 3]
                Xs = s_scr2[q4 % 2]
                mm_ci = {}
                for ci in range(4):
                    mmw = [sbankA[ci]]
                    if ci == 0 and qi == 0:
                        mmw += [projdone["Q"], projdone["K"]] + sq_all
                    tmm = None
                    for k in range(DT):
                        def mm(E, ci=ci, k=k, qi=qi):
                            return E.matmul(bank(2 + ci),
                                            QT[:, k * S + qi * 128: k * S + (qi + 1) * 128],
                                            KT_[:, k * S + ci * 512: k * S + (ci + 1) * 512],
                                            start=(k == 0), stop=False)
                        tmm = b.op("t", mm, mmw if k == 0 else [], inc=None)
                    tmm = b.op("t", lambda E, ci=ci:
                               E.matmul(bank(2 + ci), ones_col[0:1, :],
                                        ksqh[0:1, ci * 512:(ci + 1) * 512], start=False, stop=False),
                               [], inc=None)
                    tmm = b.op("t", lambda E, ci=ci, qi=qi:
                               E.matmul(bank(2 + ci), qsqh[0:1, qi * 128:(qi + 1) * 128],
                                        ones_row[0:1, :], start=False, stop=True),
                               [], inc=sT)
                    mm_ci[ci] = tmm
                for _w in range(16):
                    b.op("t", lambda E: E.ldweights(eye_bf[:]), [])
                e1_toks = []
                for ci in range(4):
                    w_ = [mm_ci[ci]]
                    if ci == 0:
                        w_.append(scr_free[q4 % 2])
                    def ex(E, ci=ci, Xs=Xs):
                        return E.activation(Xs[:, ci * 512:(ci + 1) * 512],
                                            PSA[:, (2 + ci) * 512:(3 + ci) * 512],
                                            AF.Exp, bias=0.0, scale=2.0)
                    et = b.op("s", ex, w_, inc=sS)
                    e1_toks.append(et)
                    sbankA[ci] = et
                abuf = a_bfa[par]
                e3 = b.op("s", lambda E, abuf=abuf, Xs=Xs, acc_=den:
                          E.activation(abuf[:], Xs[:], AF.Exp, bias=0.0, scale=1.0, accum_out=acc_),
                          e1_toks + [statg[par], abf_free[par]], inc=sS)
                scr_free[q4 % 2] = e3
                e4 = b.op("v", lambda E, o=rr, i=den: E.reciprocal(o, i), [e3], inc=sV)
                e5 = b.op("v", lambda E, abuf=abuf, s1=rr:
                          E.tensor_scalar(abuf[:], abuf[:], s1, None, ALU.mult), [e4], inc=sV)
                statg[par] = e5
                for g4 in range(4):
                    tri = (qi + g4) % 2
                    trt = []
                    for j in range(4):
                        kb = g4 * 4 + j
                        w_ = [e5] if j == 0 else []
                        if j == 0 and tr_free[tri] is not None:
                            w_.append(tr_free[tri])
                        trt.append(b.op("t", lambda E, kb=kb, j=j, tri=tri, abuf=abuf:
                                        E.transpose(tr_bf[tri][:, j * 128:(j + 1) * 128],
                                                    abuf[:, kb * 128:(kb + 1) * 128], eye_bf[:]),
                                        w_, inc=sT))
                    def ev(E, tri=tri, g4=g4, q4=q4):
                        dst = aT[:].rearrange("p (k c) -> p k c", k=TT)[:, g4 * 4:g4 * 4 + 4,
                                                                       q4 * 128:(q4 + 1) * 128]
                        src = tr_bf[tri][:, 0:512].rearrange("p (j c) -> p j c", j=4)
                        return E.tensor_copy(dst, src)
                    evt = b.op("v", ev, [trt[-1], prev_o], inc=sV)
                    tr_free[tri] = evt
                    atr_evs_c.append(evt)
                abf_free[par] = atr_evs_c[-1]
            o_evs = []
            o_mm_last = None
            for m in range(DT):
                pbk = m % 2
                mmw = [atr_evs_c[-1], projdone["V"], pb2f[pbk]]
                tmm = None
                for kt in range(TT):
                    def mm(E, kt=kt, m=m, pbk=pbk):
                        return E.matmul(bank(pbk),
                                        VA[:, kt * D + m * 128: kt * D + (m + 1) * 128],
                                        aT[:, kt * 512:(kt + 1) * 512],
                                        start=(kt == 0), stop=(kt == TT - 1))
                    tmm = b.op("t", mm, mmw if kt == 0 else [], inc=sT if kt == TT - 1 else None)
                o_mm_last = tmm
                evt = b.op("v", lambda E, m=m, pbk=pbk:
                           E.tensor_copy(oa_chunk[:, m * 512:(m + 1) * 512], bank(pbk)), [tmm], inc=sV)
                pb2f[pbk] = evt
                o_evs.append(evt)
            st_evs = []
            for m in range(DT):
                pbk = m % 2
                mmw = [o_evs[-1], tok_wo, pb2f[pbk]]
                if m == 0 and awo_last is not None:
                    mmw.append(awo_last)
                tmm = None
                for k in range(DT):
                    def mm(E, k=k, m=m, pbk=pbk):
                        return E.matmul(bank(pbk),
                                        wbuf[:, k * D + m * 128: k * D + (m + 1) * 128],
                                        oa_chunk[:, k * 512:(k + 1) * 512],
                                        start=(k == 0), stop=(k == DT - 1))
                    tmm = b.op("t", mm, mmw if k == 0 else [], inc=sT if k == DT - 1 else None)
                def ev(E, m=m, pbk=pbk):
                    return E.activation(awo_stage[:, m * 512:(m + 1) * 512],
                                        bank(pbk), AF.Identity,
                                        bias=vecs[:, VC["bo"] + m:VC["bo"] + m + 1], scale=1.0)
                w_ = [tmm]
                if awo_last is not None:
                    w_.append(awo_last)
                evt = b.op("s", ev, w_, inc=sS)
                pb2f[pbk] = evt
                st_evs.append(evt)
            awo_last = b.dma("q", awo_d.rearrange("(k p) s -> p k s", p=128)[:, :, qc * 512:(qc + 1) * 512],
                             awo_stage[:].rearrange("p (k c) -> p k c", k=DT),
                             dsp, waits=st_evs + [(dsp, dsp.n)])
            prev_o = o_mm_last
        return awo_last

    def phase_asm(bb, pa_done, pb3_done):
        gl_evs = []
        moein_last = None
        acc_toks = {}
        vchain = None
        prev_chunk_reads = []
        stg_free = {}
        for cc in range(4):
            ch_evs = []
            this_chunk_first = True
            for t4 in range(4):
                t = cc * 4 + t4
                par = t % 2
                ldw = [pb3_done, pa_done, stg_free.get(("x", par)), (dxt[par], dxt[par].n)]
                tok_x = b.op("q", lambda E, t=t, par=par:
                             E.dma_start_transpose(xres_tok[:, par * D:(par + 1) * D],
                                                   xres_d[:, t * 128:(t + 1) * 128]),
                             ldw, inc=dxt[par], k=16)
                tok_a = b.op("q", lambda E, t=t, par=par:
                             E.dma_start_transpose(awo_tok[:, par * D:(par + 1) * D],
                                                   awo_d[:, t * 128:(t + 1) * 128]),
                             [stg_free.get(("a", par))], inc=dxt[par], k=16)
                xtok = xres_tok[:, par * D:(par + 1) * D]
                atok = awo_tok[:, par * D:(par + 1) * D]
                a1 = b.op("v", lambda E, t=t, xtok=xtok, atok=atok:
                          E.tensor_tensor(acc[:, t * D:(t + 1) * D], xtok, atok, ALU.add),
                          [tok_x, tok_a, vchain], inc=sV)
                acc_toks[t] = a1
                stg_free[("a", par)] = a1
                vchain = a1
                if "noln" in stages:
                    stg_free[("x", par)] = a1
                    ch_evs.append(a1)
                    continue
                sumx = stats[:, 0:1]; negm = stats[:, 1:2]; vsum = stats[:, 2:3]; rstd = stats[:, 3:4]
                l1 = b.op("v", lambda E, o=sumx, xtok=xtok: E.reduce_sum(o, xtok, axis=AX.X), [a1], inc=sV)
                l2 = b.op("v", lambda E, o=negm, i=sumx: E.tensor_scalar(o, i, -1.0 / D, None, ALU.mult),
                          [l1], inc=sV)
                l3 = b.op("v", lambda E, o=c_f[:], s1=negm, xtok=xtok:
                          E.tensor_scalar(o, xtok, s1, None, ALU.add), [l2], inc=sV)
                stg_free[("x", par)] = l3
                l4 = b.op("v", lambda E, o=jnk_f[:], i=c_f[:], a=vsum:
                          E.scalar_tensor_tensor(o, i, 1.0, i, ALU.mult, ALU.mult, accum_out=a),
                          [l3], inc=sV)
                l5 = b.op("s", lambda E, o=rstd, i=vsum: E.activation(o, i, AF.Ln, bias=eps_col[:], scale=1.0 / D),
                          [l4], inc=sS)
                l6 = b.op("s", lambda E, o=rstd, i=rstd: E.activation(o, i, AF.Exp, bias=0.0, scale=-0.5),
                          [l5], inc=sS)
                l7 = b.op("v", lambda E, o=nrm_f[:], i=c_f[:], s1=rstd: E.tensor_scalar(o, i, s1, None, ALU.mult),
                          [l6], inc=sV)
                vchain = l7
                if "noltr" in stages:
                    stg_free[("x", par)] = l3
                    ch_evs.append(l7)
                    continue
                for g4 in range(2):
                    tri = (t * 2 + g4) % 2
                    trt = []
                    for j in range(4):
                        kk = g4 * 4 + j
                        w_ = [l7] if j == 0 else []
                        if j == 0 and tr_free[tri] is not None:
                            w_.append(tr_free[tri])
                        if j == 0 and this_chunk_first:
                            w_ += prev_chunk_reads
                        trt.append(b.op("t", lambda E, kk=kk, j=j, tri=tri:
                                        E.transpose(PSA[:, (6 + tri) * 512 + j * 128:(6 + tri) * 512 + (j + 1) * 128],
                                                    nrm_f[:, kk * 128:(kk + 1) * 128], eye_f[:]),
                                        w_, inc=sT))
                    this_chunk_first = False
                    evt = None
                    for j in range(4):
                        kk = g4 * 4 + j
                        if "nomoee" in stages:
                            def ev(E, tri=tri, j=j, kk=kk, t4=t4):
                                return E.tensor_copy(
                                    moechunk[:, kk * 512 + t4 * 128: kk * 512 + (t4 + 1) * 128],
                                    PSA[:, (6 + tri) * 512 + j * 128:(6 + tri) * 512 + (j + 1) * 128])
                        else:
                            def ev(E, tri=tri, j=j, kk=kk, t4=t4):
                                return E.tensor_scalar(
                                    moechunk[:, kk * 512 + t4 * 128: kk * 512 + (t4 + 1) * 128],
                                    PSA[:, (6 + tri) * 512 + j * 128:(6 + tri) * 512 + (j + 1) * 128],
                                    vecs[:, VC["ln2_g"] + kk:VC["ln2_g"] + kk + 1],
                                    vecs[:, VC["ln2_b"] + kk:VC["ln2_b"] + kk + 1],
                                    ALU.mult, ALU.add)
                        w_ = [trt[-1], vchain] if evt is None else [trt[-1], evt]
                        evt = b.op("v", ev, w_, inc=sV)
                    tr_free[tri] = evt
                    vchain = evt
                    ch_evs.append(evt)
            if "noln" in stages or "noltr" in stages:
                gl_evs.append(None)
                moein_last = ch_evs[-1]
                continue
            g1t = None
            for k in range(DT):
                def mm(E, k=k):
                    return E.matmul(PSA[0:8, 0:512], wg_t[:, k * NEXP:(k + 1) * NEXP],
                                    moechunk[:, k * 512:(k + 1) * 512],
                                    start=(k == 0), stop=(k == DT - 1))
                g1t = b.op("t", mm, ([tok_const] + ch_evs) if k == 0 else [],
                           inc=sT if k == DT - 1 else None)
            g2t = b.op("v", lambda E, cc=cc: E.tensor_copy(glT[0:8, cc * 512:(cc + 1) * 512],
                                                           PSA[0:8, 0:512]), [g1t, vchain], inc=sV)
            gl_evs.append(g2t)
            vchain = g2t
            prev_chunk_reads = [g1t]
            for k in range(DT):
                def cp(E, k=k, cc=cc):
                    return E.tensor_copy(moeinT[:, k * S + cc * 512: k * S + (cc + 1) * 512],
                                         moechunk[:, k * 512:(k + 1) * 512])
                moein_last = b.op("v", cp, [ch_evs[-1], vchain], inc=sV)
                vchain = moein_last
            prev_chunk_reads.append(moein_last)
        gate_toks = {}
        if "nogate" in stages:
            return moein_last, gate_toks, acc_toks, vchain
        for t in range(TT):
            w_ = [gl_evs[t // 4], tr_free[0], vchain]
            tt_ = b.op("t", lambda E, t=t: E.transpose(PSA[:, 6 * 512:6 * 512 + 8],
                                                       glT[0:8, t * 128:(t + 1) * 128],
                                                       eye_f[0:8, 0:8]), w_, inc=sT)
            gv = b.op("v", lambda E, t=t: E.tensor_copy(gl_tok[:, t * NEXP:(t + 1) * NEXP],
                                                        PSA[:, 6 * 512:6 * 512 + 8]), [tt_], inc=sV)
            tr_free[0] = gv
            vchain = gv
            glt = gl_tok[:, t * NEXP:(t + 1) * NEXP]
            negm1 = stats[:, 4:5]; m2 = stats[:, 5:6]; ed = stats[:, 6:7]; rr2 = stats[:, 7:8]
            q1 = b.op("v", lambda E, o=negm1, i=glt: E.reduce_max(o, i, axis=AX.X, negate=True),
                      [gv, vchain], inc=sV)
            q2 = b.op("v", lambda E, i=glt, s1=negm1: E.tensor_scalar(gjunk[:, 0:NEXP], i, s1, None, ALU.add),
                      [q1], inc=sV)
            q4 = b.op("v", lambda E: E.tensor_scalar(gjunk[:, 8:8 + NEXP], gjunk[:, 0:NEXP], 0.0, None, ALU.is_ge),
                      [q2], inc=sV)
            q5 = b.op("v", lambda E, i=glt: E.scalar_tensor_tensor(gjunk[:, 16:16 + NEXP], gjunk[:, 8:8 + NEXP],
                                                                   NEG, i, ALU.mult, ALU.add),
                      [q4], inc=sV)
            q6 = b.op("v", lambda E, o=m2: E.reduce_max(o, gjunk[:, 16:16 + NEXP], axis=AX.X), [q5], inc=sV)
            q7 = b.op("v", lambda E, i=glt, s1=m2: E.tensor_scalar(gjunk[:, 8:8 + NEXP], i, s1, None, ALU.is_ge),
                      [q6], inc=sV)
            q8 = b.op("s", lambda E, i=glt, bias=negm1: E.activation(gjunk[:, 16:16 + NEXP], i, AF.Exp,
                                                                    bias=bias, scale=1.0),
                      [q6], inc=sS)
            q9 = b.op("s", lambda E, i=m2, bias=negm1, o=ed: E.activation(o, i, AF.Exp, bias=bias, scale=1.0),
                      [q6], inc=sS)
            q10 = b.op("v", lambda E, o=ed, i=ed: E.tensor_scalar(o, i, 1.0, None, ALU.add), [q9], inc=sV)
            q11 = b.op("v", lambda E, o=rr2, i=ed: E.reciprocal(o, i), [q10], inc=sV)
            q12 = b.op("v", lambda E: E.tensor_tensor(gjunk[:, 16:16 + NEXP], gjunk[:, 16:16 + NEXP],
                                                      gjunk[:, 8:8 + NEXP], ALU.mult), [q7, q8], inc=sV)
            q13 = b.op("v", lambda E, t=t, s1=rr2: E.tensor_scalar(gate_w[:, t * NEXP:(t + 1) * NEXP],
                                                                   gjunk[:, 16:16 + NEXP], s1, None, ALU.mult),
                       [q11, q12], inc=sV)
            gate_toks[t] = q13
            vchain = q13
        return moein_last, gate_toks, acc_toks, vchain

    def phase_moe(bb, moein_done, gate_toks, acc_toks, asm_done):
        w1_use_done = None
        last_acc = dict(acc_toks)
        prev_w2_evs = None
        w2slot_free = None
        nexp_run = 1 if "moe1" in stages else NEXP
        bias_done = {}
        for e in range(nexp_run):
            w1w = [cast_toks[f"w1_{e}"], (dw1, dw1.n)]
            if e == 0:
                w1w += [moein_done, asm_done]
            if w1_use_done is not None:
                w1w.append(w1_use_done)
            b.dma("q", w1a[:].rearrange("p (k f) -> p k f", k=4),
                  w1_s[e].rearrange("(k p) f -> p k f", p=128)[:, 0:4, :], dw1, waits=w1w)
            tok_w1 = b.dma("q", w1b[:].rearrange("p (k f) -> p k f", k=4),
                           w1_s[e].rearrange("(k p) f -> p k f", p=128)[:, 4:8, :], dw1)
            sl = e % 2
            tok_b1 = b.dma("q", b1e[:, sl * FT:(sl + 1) * FT],
                           b1_h[e].rearrange("(k p) -> p k", p=128), dsmall,
                           waits=[(dsmall, dsmall.n)], allow_slow_non_contiguous=True)
            b2w = [(dcast, dcast.n), (dsmall, dsmall.n)]
            if e >= 2:
                b2w.append(bias_done[e - 2])
            tok_b2 = b.dma("g", b2e_flat[0:1, sl * D:(sl + 1) * D],
                           b2_h[e].rearrange("(a d) -> a d", a=1), dsmall, waits=b2w)
            for g in range(NG):
                gelu_toks = {}
                pb_free = [None, None]
                for m in range(FT):
                    pbk = m % 2
                    mmw = [tok_w1, moein_done, pb_free[pbk]]
                    if m < 2 and prev_w2_evs is not None:
                        mmw.append(prev_w2_evs[pbk])
                    tmm = None
                    for k in range(DT):
                        def mm(E, k=k, m=m, g=g, pbk=pbk):
                            wsrc = w1a if k < 4 else w1b
                            kk = k % 4
                            return E.matmul(bank(pbk),
                                            wsrc[:, kk * F + m * 128: kk * F + (m + 1) * 128],
                                            moeinT[:, k * S + g * 512: k * S + (g + 1) * 512],
                                            start=(k == 0), stop=(k == DT - 1))
                        tmm = b.op("t", mm, mmw if k == 0 else [], inc=sT if k == DT - 1 else None)
                    gt = b.op("s", lambda E, m=m, pbk=pbk, sl=sl:
                              E.activation(h1[:, m * 512:(m + 1) * 512], bank(pbk),
                                           AF.Gelu_apprx_tanh,
                                           bias=b1e[:, sl * FT + m: sl * FT + m + 1], scale=1.0),
                              [tmm, tok_b1], inc=sS)
                    pb_free[pbk] = gt
                    gelu_toks[m] = gt
                def w2_dma(j):
                    par = j % 2
                    w_ = [cast_toks[f"w2_{e}"], (dw2[par], dw2[par].n)]
                    if w2slot_free is not None:
                        w_.append(w2slot_free[j] if isinstance(w2slot_free, dict) else w2slot_free)
                    if j >= 2:
                        w_.append(chunk_done[j - 2])
                    return b.dma("q",
                                 w2win[:].rearrange("p (j2 d) -> p j2 d", j2=8)[:, par * 4:(par + 1) * 4, :],
                                 w2_s[e].rearrange("(kt p) d -> p kt d", p=128)[:, j * 4:(j + 1) * 4, :],
                                 dw2[par], waits=w_)
                chunk_done = {}
                w2ch = {}
                w2ch[0] = w2_dma(0)
                w2ch[1] = w2_dma(1)
                tmm = None
                for j in range(8):
                    par = j % 2
                    for k4 in range(4):
                        k = j * 4 + k4
                        for tl in range(4):
                            for ncc in range(2):
                                bi = tl * 2 + ncc
                                mmw = []
                                if tl == 0 and ncc == 0:
                                    mmw = [w2ch[j], gelu_toks[k]]
                                    if k == 0:
                                        mmw.append(gelu_toks[FT - 1])
                                if k == 0 and prev_w2_evs is not None:
                                    mmw.append(prev_w2_evs[bi])
                                def mm(E, k=k, par=par, k4=k4, tl=tl, ncc=ncc, bi=bi):
                                    return E.matmul(bank(bi),
                                                    h1[:, k * 512 + tl * 128: k * 512 + (tl + 1) * 128],
                                                    w2win[:, (par * 4 + k4) * D + ncc * 512:
                                                          (par * 4 + k4) * D + (ncc + 1) * 512],
                                                    start=(k == 0), stop=False)
                                last = (k4 == 3 and tl == 3 and ncc == 1)
                                tmm = b.op("t", mm, mmw, inc=sT if last else None)
                    chunk_done[j] = tmm
                    if j + 2 < 8:
                        w2ch[j + 2] = w2_dma(j + 2)
                bias_mm = None
                for bi in range(8):
                    ncc = bi % 2
                    def mm(E, bi=bi, ncc=ncc, sl=sl):
                        return E.matmul(bank(bi), ones_col[0:1, :],
                                        b2e_flat[0:1, sl * D + ncc * 512: sl * D + (ncc + 1) * 512],
                                        start=False, stop=True)
                    bias_mm = b.op("t", mm, [tok_b2] if bi == 0 else [], inc=sT)
                w2slot_free = {j: chunk_done[6 + (j % 2)] for j in range(8)}
                bias_done[e] = bias_mm
                evs = []
                for bi in range(8):
                    tl, ncc = bi // 2, bi % 2
                    t_glob = g * 4 + tl
                    def ev(E, bi=bi, ncc=ncc, t_glob=t_glob, e=e):
                        gcol = gate_w[:, t_glob * NEXP + e: t_glob * NEXP + e + 1]
                        dst = acc[:, t_glob * D + ncc * 512: t_glob * D + (ncc + 1) * 512]
                        return E.scalar_tensor_tensor(dst, bank(bi), gcol, dst, ALU.mult, ALU.add)
                    evt = b.op("v", ev, [bias_mm, gate_toks[t_glob], last_acc[t_glob]], inc=sV)
                    last_acc[t_glob] = evt
                    evs.append(evt)
                prev_w2_evs = evs
                if g == NG - 1:
                    w1_use_done = gelu_toks[FT - 1]
        return last_acc

    def phase_fin(bb, last_acc):
        all_moe = [last_acc[t] for t in range(TT)]
        prev_store = {}
        last_store = None
        for c in range(4):
            par = c % 2
            w_ = list(all_moe)
            if c >= 2 and prev_store.get(c - 2) is not None:
                w_.append(prev_store[c - 2])
            tok_ld = b.dma("q", fxstage[par][:].rearrange("p (tt d) -> p tt d", tt=4),
                           x_h[bb, c * 512:(c + 1) * 512, :].rearrange("(tt p) d -> p tt d", p=128),
                           dx[par], waits=w_)
            add_toks = []
            for tc in range(4):
                t = c * 4 + tc
                def ad(E, par=par, tc=tc, t=t):
                    return E.tensor_tensor(ostage[par][:, tc * D:(tc + 1) * D],
                                           fxstage[par][:, tc * D:(tc + 1) * D],
                                           acc[:, t * D:(t + 1) * D], ALU.add)
                w2_ = [tok_ld, last_acc[t]]
                if c >= 2 and prev_store.get(c - 2) is not None:
                    w2_.append(prev_store[c - 2])
                add_toks.append(b.op("v", ad, w2_, inc=sV))
            st_ = b.dma("g", y_h[bb, c * 512:(c + 1) * 512, :].rearrange("(tt p) d -> p tt d", p=128),
                        ostage[par][:].rearrange("p (tt d) -> p tt d", tt=4),
                        dout, waits=add_toks)
            prev_store[c] = st_
            last_store = st_
        return last_store

    batch_done = None
    for bb in range(nbatch):
        p0_done = phase_p0(bb, batch_done) if "p0" in stages else None
        if bb == 0:
            record_big_casts(p0_done)
        if p0_done is not None:
            tap(f"xnT{bb}", xnT[:], [p0_done])
            tap(f"hT{bb}", hT[:], [p0_done])
        pb2_done = phase_pb2(bb, p0_done) if "pb2" in stages else None
        if pb2_done is not None:
            tap(f"oT{bb}", oT[:], [pb2_done])
        pb3_done = phase_pb3(bb, pb2_done, p0_done) if "pb3" in stages else None
        if pb3_done is not None:
            tap(f"xresD{bb}", xres_d, [pb3_done])
        pa_done = phase_pa(bb, pb3_done, p0_done) if "pa" in stages else None
        if pa_done is not None:
            tap(f"awoD{bb}", awo_d, [pa_done])
        if "asm" in stages:
            moein_done, gate_toks, acc_toks, asm_done = phase_asm(bb, pa_done, pb3_done)
            tap(f"moeinT{bb}", moeinT[:], [moein_done, asm_done])
            tap(f"gate{bb}", gate_w[:], [asm_done])
            batch_done = asm_done
        if "moe" in stages:
            last_acc = phase_moe(bb, moein_done, gate_toks, acc_toks, asm_done)
            tap(f"accT{bb}", acc[:], [last_acc[TT - 1]])
            batch_done = last_acc[TT - 1]
        if "fin" in stages:
            batch_done = phase_fin(bb, last_acc)

    final_waits = [(dout, dout.n)]
    if batch_done is not None:
        final_waits.append(batch_done)
    b.op("g", lambda E: E.engine_nop(), final_waits)

    with nc.Block() as block:
        b.finish(block)
    es.close()
    return nc, tap_names


NAME_MAP = {
    "x": "L0_attn_x", "Wq": "L0_attn_Wq", "bq": "L0_attn_bq",
    "Wk": "L0_attn_Wk", "bk": "L0_attn_bk", "Wv": "L0_attn_Wv",
    "bv": "L0_attn_bv", "Wo": "L0_attn_Wo", "bo": "L0_attn_bo",
    "ln1_g": "L0_ffn_ln1_g", "ln1_b": "L0_ffn_ln1_b",
    "ln2_g": "L0_ffn_ln2_g", "ln2_b": "L0_ffn_ln2_b",
    "w_qkv": "L0_ffn_w_qkv", "w_o": "L0_ffn_w_o", "w_gate": "L0_ffn_w_gate",
    "w1": "L0_ffn_w1", "b1": "L0_ffn_b1", "w2": "L0_ffn_w2", "b2": "L0_ffn_b2",
}


def make_in_maps(inputs, n_cores=8):
    x = np.ascontiguousarray(np.asarray(inputs[NAME_MAP["x"]], dtype=np.float32))
    base = {k: np.ascontiguousarray(np.asarray(inputs[v], dtype=np.float32))
            for k, v in NAME_MAP.items() if k != "x"}
    in_maps = []
    for c in range(n_cores):
        m = dict(base)
        m["x"] = x[c * B_LOC:(c + 1) * B_LOC]
        in_maps.append(m)
    return in_maps


def kernel(**inputs):
    nc, _ = build()
    in_maps = make_in_maps(inputs)
    res = run_bass_kernel_spmd(nc, in_maps, core_ids=list(range(8)))
    return np.concatenate([r["y"] for r in res.results], axis=0)
